# version 08
# speedup vs baseline: 1.1200x; 1.0543x over previous
import numpy as np
import ml_dtypes
import concourse.bass as bass
import concourse.mybir as mybir
from concourse.bass_utils import run_bass_kernel_spmd

F32 = mybir.dt.float32
BF16 = mybir.dt.bfloat16
U8 = mybir.dt.uint8
AF = mybir.ActivationFunctionType
ALU = mybir.AluOpType
AX = mybir.AxisListType

B, L, D, H, DH, NB, MLPD, C = 4, 4096, 1024, 16, 64, 32, 4096, 64
NCORE = 8
TOK = B * L // NCORE
HALO = 8
NCHUNK = L // C
UPC = B * H // NCORE

BF = ml_dtypes.bfloat16


def _bf(x):
    return np.ascontiguousarray(np.asarray(x, dtype=np.float32)).astype(BF)


def _f32(x):
    return np.ascontiguousarray(np.asarray(x), dtype=np.float32)


def _chunk_consts():
    ts = (np.arange(NCHUNK) * C).astype(np.float64)
    idx = np.arange(1, C + 1, dtype=np.float64)[None, :] + ts[:, None]
    cs = np.cumsum(idx, axis=1)
    widx = idx + idx.sum(1, keepdims=True) - cs
    w = widx / ((1 + ts + C)[:, None] * (ts + C)[:, None] / 2.0)
    with np.errstate(divide="ignore", invalid="ignore"):
        c1 = np.where(ts > 0, ts / (ts + C) * ((1 + ts) / (1 + ts + C)), 0.0)
    c2 = C / (ts + C) * ((ts * 2 + 1 + C) / (1 + ts + C))
    tri0 = (1 + ts) * ts / 2
    a = tri0[:, None] / (tri0[:, None] + cs)
    r = cs[:, :, None]
    j_idx = (cs - idx)[:, None, :]
    Tmat = np.where(r >= j_idx, (r - j_idx) / (tri0[:, None, None] + r), 0.0)
    return (w.astype(np.float32), c1.astype(np.float32), c2.astype(np.float32),
            a.astype(np.float32), Tmat.astype(np.float32))


W_TAB, C1_TAB, C2_TAB, A_TAB, TMAT_TAB = _chunk_consts()
MASK_SL = (np.arange(C)[:, None] > np.arange(C)[None, :]).astype(np.float32)
IDENT64 = np.eye(C, dtype=np.float32)
E16 = np.zeros((D, H), dtype=np.float32)
for _h in range(H):
    E16[_h * DH:(_h + 1) * DH, _h] = 1.0


PUB = {"tensor": "s_pe", "vector": "s_v", "scalar": "s_sc", "gpsimd": "s_gp", "sync": None}


def _flat_instrs(x, out):
    if x is None:
        return
    if isinstance(x, (list, tuple)):
        for y in x:
            _flat_instrs(y, out)
    elif hasattr(x, "then_inc"):
        out.append(x)


class Prog:
    def __init__(self, name):
        self.nc = bass.Bass("TRN2", target_bir_lowering=False, debug=False)
        self.name = name
        self._cms = []
        self._phase_cms = []
        self.sems = {}
        self.ct = {}
        self.t = {}
        self.pre = ""
        self._block_cm = None
        self.block = None

    def sem(self, *names):
        for n in names:
            cm = self.nc.semaphore(n)
            self.sems[n] = cm.__enter__()
            self._cms.append(cm)
            self.ct[n] = 0

    def sb(self, name, shape, dtype):
        name = self.pre + name
        cm = self.nc.sbuf_tensor(name, list(shape), dtype)
        h = cm.__enter__()
        self._phase_cms.append((name, cm))
        self.t[name] = h
        return h

    def sbr(self, name, shape, dtype):
        if self.pre + name in self.t:
            return self.t[self.pre + name]
        return self.sb(name, shape, dtype)

    def psr(self, name, shape, dtype=F32):
        if self.pre + name in self.t:
            return self.t[self.pre + name]
        return self.ps(name, shape, dtype)

    def ps(self, name, shape, dtype=F32):
        name = self.pre + name
        cm = self.nc.psum_tensor(name, list(shape), dtype)
        h = cm.__enter__()
        self._phase_cms.append((name, cm))
        self.t[name] = h
        return h

    def din(self, name, shape, dtype):
        return self.nc.dram_tensor(name, list(shape), dtype, kind="ExternalInput").ap()

    def dout(self, name, shape, dtype):
        return self.nc.dram_tensor(name, list(shape), dtype, kind="ExternalOutput").ap()

    def dtmp(self, name, shape, dtype):
        return self.nc.dram_tensor(name, list(shape), dtype).ap()

    def end_phase(self):
        for name, cm in reversed(self._phase_cms):
            cm.__exit__(None, None, None)
            self.t.pop(name, None)
        self._phase_cms = []

    def phase_barrier(self):
        snap = dict(self.ct)
        names = [n for n in snap if snap[n] > 0]
        for eng in ("tensor", "vector", "scalar", "gpsimd", "sync"):
            pn = PUB[eng]

            def fn(e, pn=pn):
                for n in names:
                    if n != pn:
                        e.wait_ge(self.sems[n], snap[n])
                if pn is not None:
                    e.drain().then_inc(self.sems[pn], 1)
            getattr(self.block, eng)(fn)
            if pn is not None:
                self.ct[pn] += 1

    def begin(self, consts=()):
        self._block_cm = self.nc.Block()
        self.block = self._block_cm.__enter__()
        self._cvals = list(consts)
        if self._cvals:
            cb = self.sb("b_cst", [128, len(self._cvals)], F32)
            nm, cm = self._phase_cms.pop()
            self._cms.append(cm)
            self.seg("vector", lambda v: [v.memset(cb[:, i:i + 1], float(val))
                                          for i, val in enumerate(self._cvals)])
            self._cst_ready = self.ct["s_v"]

    def cst(self, val, prows=128):
        i = self._cvals.index(val)
        return self.t["b_cst"][0:prows, i:i + 1]

    def seg(self, eng, body, waits=(), pub=True, fuse=None):
        sems = self.sems
        wl = [(sems[n], v) for (n, v) in waits if v]
        pn = PUB[eng] if pub else None
        if fuse is None:
            fuse = eng in ("tensor", "vector", "scalar")
        nn = [0]

        def fn(e):
            for h, v in wl:
                e.wait_ge(h, v)
            r = body(e)
            if pn is None:
                return
            instrs = []
            _flat_instrs(r, instrs)
            if fuse and instrs:
                if eng == "tensor":
                    instrs[-1].then_inc(sems[pn], 1)
                    nn[0] = 1
                else:
                    for i in instrs:
                        i.then_inc(sems[pn], 1)
                    nn[0] = len(instrs)
            else:
                e.drain().then_inc(sems[pn], 1)
                nn[0] = 1
        getattr(self.block, eng)(fn)
        if pn:
            self.ct[pn] += nn[0]
            return self.ct[pn]
        return None

    def dma(self, eng, pairs, sem, waits=(), single_packet=False, transpose=False):
        h = self.sems[sem]
        wl = [(self.sems[n], v) for (n, v) in waits if v]
        prev_total = self.ct[sem]

        def fn(e):
            if prev_total:
                e.wait_ge(h, prev_total)
            for hh, v in wl:
                e.wait_ge(hh, v)
            for o, i in pairs:
                if transpose:
                    e.dma_start_transpose(out=o, in_=i).then_inc(h, 16)
                else:
                    e.dma_start(out=o, in_=i, single_packet=single_packet).then_inc(h, 16)
        getattr(self.block, eng)(fn)
        self.ct[sem] += 16 * len(pairs)
        return self.ct[sem]

    def idma(self, items, sem, waits=()):
        h = self.sems[sem]
        wl = [(self.sems[n], v) for (n, v) in waits if v]
        prev_total = self.ct[sem]

        def fn(e):
            if prev_total:
                e.wait_ge(h, prev_total)
            for hh, v in wl:
                e.wait_ge(hh, v)
            for out, in_, idx in items:
                e.indirect_dma_start(
                    out=out, out_offset=None, in_=in_,
                    in_offset=bass.IndirectOffsetOnAxis(idx, axis=0)).then_inc(h, 16)
        self.block.gpsimd(fn)
        self.ct[sem] += 16 * len(items)
        return self.ct[sem]

    def collective(self, ins, outs, waits=()):
        wl = [(self.sems[n], v) for (n, v) in waits if v]
        h = self.sems["s_cc"]

        def fn(e):
            for hh, v in wl:
                e.wait_ge(hh, v)
            e.collective_compute("AllGather", ALU.bypass, replica_groups=PAIRS,
                                 ins=ins, outs=outs).then_inc(h, 1)
        self.block.gpsimd(fn)
        self.ct["s_cc"] += 1
        return self.ct["s_cc"]

    def barrier(self, st_sem, st_val, engines=("tensor", "vector", "scalar", "sync")):
        for e in engines:
            if e == "sync":
                self.seg("sync", lambda s: s.wait_ge(self.sems[st_sem], st_val), pub=False)
            else:
                self.seg(e, lambda _e: None, waits=[(st_sem, st_val)])

    def finish(self):
        self._block_cm.__exit__(None, None, None)
        for cm in reversed(self._cms):
            cm.__exit__(None, None, None)
        return self.nc



def dio(p, io, name, shape, dtype, out=False):
    if io is not None and name in io:
        return io[name]
    return (p.dout if out else p.din)(name, shape, dtype)

FUSED_CONSTS = [1e-5, 1e-6, 1e-11, 1e-30]
FUSED_SEMS = ("s_ld", "s_g", "s_g0", "s_g1", "s_w0", "s_w1", "s_bk", "s_pe", "s_v",
              "s_sc", "s_gp", "s_st", "s_cc")


def col_chunks(total, step=512):
    out, c = [], 0
    while c < total:
        out.append((c, min(step, total - c)))
        c += step
    return out


def ptile(ap, j, w, c0=0, cw=None):
    cw = w if cw is None else cw
    return ap[:, j * w + c0: j * w + c0 + cw]


def build_L1():
    p = Prog("L1")
    WT = HALO + TOK
    nblk = 4
    BW = TOK // nblk
    BWH = BW + HALO

    d_x = p.din("x", [D, WT], F32)
    d_wk = p.din("wkT", [D, D], F32)
    d_wq = p.din("wqT", [D, D], F32)
    d_wv = p.din("wvT", [D, D], F32)
    d_wb = p.din("wbT", [D, H], F32)
    d_bias = p.din("biases", [128, 8 * 3], F32)
    d_bbeta = p.din("bbeta", [H, 1], F32)
    d_conv = p.din("convw", [128, 8 * 4 * 3], F32)
    d_hscal = p.din("hscal", [128, 1], F32)
    d_e16 = p.din("e16", [128, 8 * H], BF16)
    d_e16t = p.din("e16t", [H, 8 * 128], F32)
    d_onesr = p.din("ones_row", [1, 128], F32)

    d_kn = p.dout("kn", [D, TOK], BF16)
    d_qn = p.dout("qn", [D, TOK], BF16)
    d_bb = p.dout("bb", [D, TOK], BF16)
    d_vv = p.dout("vv", [D, TOK], BF16)
    d_beta = p.dout("beta", [H, TOK], BF16)

    x = p.sb("b_x", [128, 8 * BWH], F32)
    xbf = p.sb("b_xbf", [128, 8 * BWH], BF16)
    xln = p.sb("b_xln", [128, 8 * BWH], BF16)
    sq = p.sb("b_sq", [128, 8 * BWH], BF16)
    wk = p.sb("b_wk", [128, 8 * D], BF16)
    wq = p.sb("b_wq", [128, 8 * D], BF16)
    wv = p.sb("b_wv", [128, 8 * D], BF16)
    wb = p.sb("b_wb", [128, 8 * H], BF16)
    bias = p.sb("b_bias", [128, 8 * 3], F32)
    bbeta = p.sb("b_bbeta", [H, 1], F32)
    convw = p.sb("b_convw", [128, 8 * 4 * 3], F32)
    hscal = p.sb("b_hscal", [128, 1], F32)
    e16 = p.sb("b_e16", [128, 8 * H], BF16)
    e16t = p.sb("b_e16t", [H, 8 * 128], F32)
    onesr = p.sb("b_ones_row", [1, 128], F32)
    onesb = p.sb("b_ones_bf", [128, 1], BF16)
    kp = p.sb("b_kp", [128, 8 * BWH], BF16)
    qp = p.sb("b_qp", [128, 8 * BWH], BF16)
    vp = p.sb("b_vp", [128, 8 * BWH], BF16)
    kc = p.sb("b_kc", [128, 8 * BW], BF16)
    qc = p.sb("b_qc", [128, 8 * BW], BF16)
    vc = p.sb("b_vc", [128, 8 * BW], BF16)
    outb = p.sb("b_outb", [128, 8 * BW], BF16)
    betas = p.sb("b_betas", [H, BWH], F32)
    betab = p.sb("b_betab", [H, BW], BF16)
    srow = p.sb("b_srow", [1, BWH], F32)
    srow1 = p.sb("b_srow1", [1, BWH], F32)
    mu = p.sb("b_mu", [1, BWH], F32)
    rs = p.sb("b_rs", [1, BWH], F32)
    vtmp = p.sb("b_vtmp", [1, BWH], F32)
    ssk = p.sb("b_ssk", [H, BW], F32)
    rr = p.sb("b_rr", [H, BW], F32)

    p_stat = p.ps("p_stat", [1, 512])
    p_stat2 = p.ps("p_stat2", [1, 512])
    p_repA = p.ps("p_repA", [128, 512])
    p_repB = p.ps("p_repB", [128, 512])
    p_mm = p.ps("p_mm", [128, 512])
    p_mm2 = p.ps("p_mm2", [128, 512])
    p_ss = p.ps("p_ss", [H, 512])

    p.sem("s_ld", "s_g", "s_pe", "s_v", "s_sc", "s_st")
    p.begin(consts=[1e-5, 1e-30])

    wl = p.dma("gpsimd", [
        (wk[:, :].rearrange("p (j o) -> p j o", j=8), d_wk.rearrange("(j p) o -> p j o", p=128)),
        (wq[:, :].rearrange("p (j o) -> p j o", j=8), d_wq.rearrange("(j p) o -> p j o", p=128)),
        (wv[:, :].rearrange("p (j o) -> p j o", j=8), d_wv.rearrange("(j p) o -> p j o", p=128)),
        (wb[:, :].rearrange("p (j h) -> p j h", j=8), d_wb.rearrange("(j p) h -> p j h", p=128))], "s_g")
    sl = p.dma("sync", [(bias[:, :], d_bias), (bbeta[:, :], d_bbeta), (convw[:, :], d_conv),
                        (hscal[:, :], d_hscal), (e16[:, :], d_e16), (e16t[:, :], d_e16t),
                        (onesr[:, :], d_onesr)], "s_ld")
    ob = p.seg("vector", lambda v: v.memset(onesb[:, :], 1.0))

    st_ct = 0
    for blk in range(nblk):
        c0 = blk * BW
        lx = p.dma("sync", [(x[:, :].rearrange("p (j w) -> p j w", j=8),
                             d_x[:, c0:c0 + BWH].rearrange("(j p) w -> p j w", p=128))], "s_ld")

        cast = p.seg("vector", lambda v: [v.tensor_copy(ptile(xbf, j, BWH), ptile(x, j, BWH))
                                          for j in range(8)], waits=[("s_ld", lx)])
        sqd = p.seg("scalar", lambda s: [s.activation(ptile(sq, j, BWH), ptile(xbf, j, BWH), AF.Square)
                                         for j in range(8)], waits=[("s_v", cast)])

        sums_done = None
        for (cc, cw) in col_chunks(BWH):
            mmv = p.seg("tensor", lambda te, cc=cc, cw=cw: (
                [te.matmul(p_stat[0:1, 0:cw], onesb[:, 0:1], ptile(xbf, j, BWH, cc, cw),
                           start=(j == 0), stop=(j == 7)) for j in range(8)],
                [te.matmul(p_stat2[0:1, 0:cw], onesb[:, 0:1], ptile(sq, j, BWH, cc, cw),
                           start=(j == 0), stop=(j == 7)) for j in range(8)]),
                waits=[("s_sc", sqd), ("s_v", max(ob, p.ct["s_v"]))])
            sums_done = p.seg("vector", lambda v, cc=cc, cw=cw: (
                v.tensor_copy(srow[0:1, cc:cc + cw], p_stat[0:1, 0:cw]),
                v.tensor_copy(srow1[0:1, cc:cc + cw], p_stat2[0:1, 0:cw])),
                waits=[("s_pe", mmv)])

        vard = p.seg("vector", lambda v: (
            v.tensor_scalar(mu[0:1, :], srow[0:1, :], 1.0 / D, None, ALU.mult),
            v.tensor_scalar(vtmp[0:1, :], srow1[0:1, :], 1.0 / D, None, ALU.mult),
            v.drain(),
            v.tensor_tensor(srow[0:1, :], mu[0:1, :], mu[0:1, :], ALU.mult),
            v.drain(),
            v.tensor_tensor(vtmp[0:1, :], vtmp[0:1, :], srow[0:1, :], ALU.subtract)),
            waits=[("s_v", sums_done)])
        rsd = p.seg("scalar", lambda s: (
            s.activation(rs[0:1, :], vtmp[0:1, :], AF.Ln, bias=p.cst(1e-5, 1)),
            s.drain(),
            s.activation(rs[0:1, :], rs[0:1, :], AF.Exp, scale=-0.5)),
            waits=[("s_v", vard)])

        xln_done = None
        for (cc, cw) in col_chunks(BWH):
            repd = p.seg("tensor", lambda te, cc=cc, cw=cw: (
                te.matmul(p_repA[:, 0:cw], onesr[0:1, :], mu[0:1, cc:cc + cw], start=True, stop=True),
                te.matmul(p_repB[:, 0:cw], onesr[0:1, :], rs[0:1, cc:cc + cw], start=True, stop=True)),
                waits=[("s_sc", rsd), ("s_ld", p.ct["s_ld"]), ("s_v", p.ct["s_v"])])
            xln_done = p.seg("vector", lambda v, cc=cc, cw=cw: (
                [v.tensor_tensor(ptile(xln, j, BWH, cc, cw), ptile(xbf, j, BWH, cc, cw),
                                 p_repA[:, 0:cw], ALU.subtract) for j in range(8)],
                v.drain(),
                [v.tensor_tensor(ptile(xln, j, BWH, cc, cw), ptile(xln, j, BWH, cc, cw),
                                 p_repB[:, 0:cw], ALU.mult) for j in range(8)]),
                waits=[("s_pe", repd)])

        evac_hist = []
        first_proj = True
        for pi, (wsb, dst) in enumerate([(wk, kp), (wq, qp), (wv, vp)]):
            for ot in range(8):
                for ci, (cc, cw) in enumerate(col_chunks(BWH)):
                    pmm = p_mm if len(evac_hist) % 2 == 0 else p_mm2
                    wts = [("s_v", xln_done), ("s_g", wl)] if first_proj else []
                    if len(evac_hist) >= 2:
                        wts = wts + [("s_sc", evac_hist[-2])]
                    first_proj = False
                    mmd = p.seg("tensor", lambda te, wsb=wsb, ot=ot, cc=cc, cw=cw, pmm=pmm: [
                        te.matmul(pmm[:, 0:cw], wsb[:, (j * 8 + ot) * 128:(j * 8 + ot + 1) * 128],
                                  ptile(xln, j, BWH, cc, cw), start=(j == 0), stop=(j == 7))
                        for j in range(8)], waits=wts)
                    ev = p.seg("scalar", lambda s, dst=dst, ot=ot, cc=cc, cw=cw, pi=pi, pmm=pmm:
                               s.activation(ptile(dst, ot, BWH, cc, cw), pmm[:, 0:cw], AF.Identity,
                                            bias=bias[:, pi * 8 + ot:pi * 8 + ot + 1]),
                               waits=[("s_pe", mmd)])
                    evac_hist.append(ev)
        projd = evac_hist[-1]

        betad = None
        for (cc, cw) in col_chunks(BWH):
            bmm = p.seg("tensor", lambda te, cc=cc, cw=cw: [
                te.matmul(p_ss[:, 0:cw], wb[:, j * H:(j + 1) * H], ptile(xln, j, BWH, cc, cw),
                          start=(j == 0), stop=(j == 7)) for j in range(8)],
                waits=[("s_sc", p.ct["s_sc"]), ("s_v", p.ct["s_v"])])
            betad = p.seg("scalar", lambda s, cc=cc, cw=cw: s.activation(
                betas[:, cc:cc + cw], p_ss[:, 0:cw], AF.Sigmoid, bias=bbeta[:, 0:1]),
                waits=[("s_pe", bmm)])
        betac = p.seg("vector", lambda v: v.tensor_copy(betab[:, :], betas[:, HALO:HALO + BW]),
                      waits=[("s_sc", betad)])

        halo_v = p.ct["s_v"]
        if blk == 0:
            halo_v = p.seg("vector", lambda v: [
                v.tensor_scalar(ptile(d2, j, BWH, HALO - 3, 3), ptile(d2, j, BWH, HALO - 3, 3),
                                hscal[:, 0:1], None, ALU.mult)
                for d2 in (kp, qp, vp) for j in range(8)],
                waits=[("s_sc", projd)])

        def conv_body(v):
            for pi2, (d1, d2) in enumerate([(kp, kc), (qp, qc), (vp, vc)]):
                for j in range(8):
                    v.tensor_scalar(ptile(d2, j, BW), ptile(d1, j, BWH, HALO - 3, BW),
                                    convw[:, j * 12 + pi2 * 4:j * 12 + pi2 * 4 + 1], None, ALU.mult)
                    v.drain()
                    for tap in range(1, 4):
                        v.scalar_tensor_tensor(ptile(d2, j, BW), ptile(d1, j, BWH, HALO - 3 + tap, BW),
                                               convw[:, j * 12 + pi2 * 4 + tap:j * 12 + pi2 * 4 + tap + 1],
                                               ptile(d2, j, BW), ALU.mult, ALU.add)
                        v.drain()
        convd = p.seg("vector", conv_body, waits=[("s_sc", projd), ("s_v", halo_v)])

        for tag, src in (("k", kc), ("q", qc)):
            sq2 = p.seg("scalar", lambda s, src=src: [
                s.activation(ptile(sq, j, BWH, 0, BW), ptile(src, j, BW), AF.Square) for j in range(8)],
                waits=[("s_v", convd)])
            ssd = p.seg("tensor", lambda te: [
                te.matmul(p_ss[:, 0:BW], e16[:, j * H:(j + 1) * H], ptile(sq, j, BWH, 0, BW),
                          start=(j == 0), stop=(j == 7)) for j in range(8)],
                waits=[("s_sc", sq2)])
            ssv = p.seg("vector", lambda v: v.tensor_copy(ssk[:, :], p_ss[:, 0:BW]),
                        waits=[("s_pe", ssd)])
            sqr = p.seg("scalar", lambda s: (
                s.activation(rr[:, :], ssk[:, :], AF.Ln, bias=p.cst(1e-30, H)),
                s.drain(),
                s.activation(rr[:, :], rr[:, :], AF.Exp, scale=0.5)),
                waits=[("s_v", ssv)])
            rin = p.seg("vector", lambda v: (
                v.tensor_scalar(rr[:, :], rr[:, :], 1e-6, None, ALU.add),
                v.drain(),
                v.reciprocal(rr[:, :], rr[:, :])),
                waits=[("s_sc", sqr)])
            last_mult = rin
            for j in range(8):
                pr = p_repA if j % 2 == 0 else p_repB
                rd = p.seg("tensor", lambda te, j=j, pr=pr: te.matmul(
                    pr[:, 0:BW], e16t[:, j * 128:(j + 1) * 128], rr[:, 0:BW], start=True, stop=True),
                    waits=[("s_v", last_mult if j < 2 else p.ct["s_v"])])
                last_mult = p.seg("vector", lambda v, j=j, pr=pr, src=src: v.tensor_tensor(
                    ptile(src, j, BW), ptile(src, j, BW), pr[:, 0:BW], ALU.mult),
                    waits=[("s_pe", rd)])
        normd = p.ct["s_v"]

        sil = p.seg("scalar", lambda s: [
            s.activation(ptile(vc, j, BW), ptile(vc, j, BW), AF.Silu) for j in range(8)],
            waits=[("s_v", convd)])

        bvd = None
        for j in range(8):
            pr = p_repA if j % 2 == 0 else p_repB
            rd = p.seg("tensor", lambda te, j=j, pr=pr: te.matmul(
                pr[:, 0:BW], e16t[:, j * 128:(j + 1) * 128], betas[:, HALO:HALO + BW],
                start=True, stop=True),
                waits=[("s_v", p.ct["s_v"]), ("s_sc", max(betad, sil))])
            bvd = p.seg("vector", lambda v, j=j, pr=pr: (
                v.tensor_tensor(ptile(outb, j, BW), ptile(kc, j, BW), pr[:, 0:BW], ALU.mult),
                v.drain(),
                v.tensor_tensor(ptile(vc, j, BW), ptile(vc, j, BW), pr[:, 0:BW], ALU.mult)),
                waits=[("s_pe", rd)])

        st_ct = p.dma("sync", [
            (d_kn[:, c0:c0 + BW].rearrange("(j p) w -> p j w", p=128), kc[:, :].rearrange("p (j w) -> p j w", j=8)),
            (d_qn[:, c0:c0 + BW].rearrange("(j p) w -> p j w", p=128), qc[:, :].rearrange("p (j w) -> p j w", j=8)),
            (d_bb[:, c0:c0 + BW].rearrange("(j p) w -> p j w", p=128), outb[:, :].rearrange("p (j w) -> p j w", j=8)),
            (d_vv[:, c0:c0 + BW].rearrange("(j p) w -> p j w", p=128), vc[:, :].rearrange("p (j w) -> p j w", j=8)),
            (d_beta[:, c0:c0 + BW], betab[:, :]),
        ], "s_st", waits=[("s_v", bvd), ("s_sc", p.ct["s_sc"])])
        p.barrier("s_st", st_ct)

    return p.finish()


def l1_host_inputs(core, x_full, wts):
    t0 = core * TOK
    xs = x_full[t0:t0 + TOK]
    l_off = t0 % L
    if l_off == 0:
        halo = np.zeros((HALO, D), np.float32)
        hs = np.zeros((128, 1), np.float32)
    else:
        halo = np.concatenate([np.zeros((HALO - 3, D), np.float32), x_full[t0 - 3:t0]], 0)
        hs = np.ones((128, 1), np.float32)
    xw = np.concatenate([halo, xs], 0)
    biases = np.stack([wts["bk"], wts["bq"], wts["bv"]], 0)
    conv = np.stack([wts["conv_k"], wts["conv_q"], wts["conv_v"]], 0)
    return {
        "x": _f32(xw.T),
        "wkT": _f32(wts["Wk"].T), "wqT": _f32(wts["Wq"].T), "wvT": _f32(wts["Wv"].T),
        "wbT": _f32(wts["Wbeta"].T),
        "biases": _f32(np.concatenate([biases[pi].reshape(8, 128).T for pi in range(3)], 1)),
        "bbeta": _f32(wts["bbeta"].reshape(H, 1)),
        "convw": _f32(np.concatenate(
            [np.stack([conv[pi][j * 128:(j + 1) * 128, tap] for pi in range(3) for tap in range(4)], 1)
             for j in range(8)], 1)),
        "hscal": hs,
        "e16": _bf(np.concatenate([E16[j * 128:(j + 1) * 128] for j in range(8)], 1)),
        "e16t": _f32(np.concatenate([E16.T[:, j * 128:(j + 1) * 128] for j in range(8)], 1)),
        "ones_row": np.ones((1, 128), np.float32),
    }


def kernel(**inputs):
    raise NotImplementedError("under construction")


WG = 8
NG = NCHUNK // WG


def build_L2(n_chunks=NCHUNK, debug=False, p=None, io=None):
    standalone = p is None
    if standalone:
        p = Prog("L2")
    d_kf = dio(p, io, "kf", [NG, C, UPC * 512], BF16)
    d_qf = dio(p, io, "qf", [NG, C, UPC * 512], BF16)
    d_km = dio(p, io, "km", [NG, C, UPC * 512], BF16)
    d_bv = dio(p, io, "bv", [NG, C, UPC * 1024], BF16)
    d_bt = dio(p, io, "betat", [C, NCHUNK * UPC], F32)
    d_tmt8 = dio(p, io, "tmt8", [NG, C, WG * 512], BF16)
    d_arep = dio(p, io, "arep", [C, NCHUNK * C], BF16)
    d_crep = dio(p, io, "crep", [C, NCHUNK * C], BF16)
    d_id64r = dio(p, io, "id64r", [C, 512], BF16)
    d_wtab = dio(p, io, "wtab", [C, NCHUNK], F32)
    d_mask = dio(p, io, "mask01", [C, C], BF16)
    d_id64 = dio(p, io, "id64", [C, C], BF16)
    d_rmsw = dio(p, io, "rmsw", [C, C], BF16)
    d_o = dio(p, io, "o", [NCHUNK, C, 512], BF16, out=True)

    kf = [p.sb("b_kf%d" % i, [C, UPC * 512], BF16) for i in range(2)]
    qf = [p.sb("b_qf%d" % i, [C, UPC * 512], BF16) for i in range(2)]
    km = [p.sb("b_km%d" % i, [C, UPC * 512], BF16) for i in range(2)]
    bv = [p.sb("b_bv%d" % i, [C, UPC * 1024], BF16) for i in range(2)]
    bt = p.sb("b_bt", [C, NCHUNK * UPC], F32)
    btn = p.sb("b_btn", [C, NCHUNK * UPC], F32)
    tmt8 = [p.sb("b_tmt8%d" % i, [C, WG * 512], BF16) for i in range(2)]
    arep = p.sb("b_arep", [C, NCHUNK * C], BF16)
    crep = p.sb("b_crep", [C, NCHUNK * C], BF16)
    id64r = p.sb("b_id64r", [C, 512], BF16)
    wtab = p.sb("b_wtab", [C, NCHUNK], F32)
    mask = p.sb("b_mask", [C, C], BF16)
    id64 = p.sb("b_id64", [C, C], BF16)
    rmsw = p.sb("b_rmsw", [C, C], BF16)
    ot = [p.sb("b_ot%d" % i, [C, 512], BF16) for i in range(2)]
    ssq = p.sb("b_ssq", [C, UPC], F32)
    rsq = p.sb("b_rsq", [C, UPC], F32)
    sqt = p.sb("b_sqt", [C, 512], BF16)

    S_ = p.sb("b_S", [C, UPC * C], BF16)
    At = [p.sb("b_At%d" % j, [C, UPC * C], BF16) for j in range(6)]
    Aj = [p.sb("b_Aj%d" % j, [C, UPC * C], BF16) for j in range(4)]
    XB = p.sb("b_XB", [C, UPC * C], BF16)
    XV = p.sb("b_XV", [C, UPC * C], BF16)
    tqk = p.sb("b_tqk", [C, UPC * C], BF16)
    kw = p.sb("b_kw", [C, UPC * C], BF16)
    negat = p.sb("b_negat", [C, UPC * C], BF16)
    gt = p.sb("b_gt", [C, UPC * C], BF16)
    negdt = p.sb("b_negdt", [C, UPC * C], BF16)
    qat = p.sb("b_qat", [C, UPC * C], BF16)
    qct = p.sb("b_qct", [C, UPC * C], BF16)
    wsb = p.sb("b_wsb", [C, UPC * C], BF16)
    wavf = [p.sb("b_wavf%d" % i, [C, UPC * C], F32) for i in range(2)]
    wavb = p.sb("b_wavb", [C, UPC * C], BF16)

    p_c0 = p.ps("p_c0", [C, 512])
    p_c1 = p.ps("p_c1", [C, 512])
    p_tr = p.ps("p_tr", [C, 512], BF16)
    p_W = p.ps("p_W", [C, 512])
    p_Wa = p.ps("p_Wa", [C, 512])
    p_O = p.ps("p_O", [C, 512])

    if standalone:
        p.sem("s_ld", "s_g", "s_pe", "s_v", "s_sc", "s_st", "s_w0", "s_w1")
        p.begin(consts=[1e-6])

    cl = p.dma("sync", [(bt[:, :], d_bt), (id64r[:, :], d_id64r), (arep[:, :], d_arep),
                        (crep[:, :], d_crep), (wtab[:, :], d_wtab), (mask[:, :], d_mask),
                        (id64[:, :], d_id64), (rmsw[:, :], d_rmsw)], "s_ld")
    init_v = p.seg("vector", lambda v: (
        v.tensor_scalar(btn[:, :], bt[:, :], -1.0, None, ALU.mult),
        v.memset(wavf[1][:, :], 0.0),
        v.memset(wsb[:, :], 0.0)), waits=[("s_ld", cl)])

    gload = {}

    def load_group(g):
        i = g % 2
        gload[g] = p.dma("sync", [
            (kf[i][:, :], d_kf[g]), (qf[i][:, :], d_qf[g]),
            (km[i][:, :], d_km[g]), (bv[i][:, :], d_bv[g]),
            (tmt8[i][:, :], d_tmt8[g])], "s_w%d" % i,
            waits=[("s_pe", p.ct["s_pe"]), ("s_v", p.ct["s_v"])] if g >= 2 else ())

    load_group(0)
    load_group(1)

    def sl_kt(g, u, c):
        return kf[g % 2][:, u * 512 + (c % WG) * 64: u * 512 + (c % WG) * 64 + 64]

    def sl_qt(g, u, c):
        return qf[g % 2][:, u * 512 + (c % WG) * 64: u * 512 + (c % WG) * 64 + 64]

    def sl_km(g, u, c):
        return km[g % 2][:, u * 512 + (c % WG) * 64: u * 512 + (c % WG) * 64 + 64]

    def sl_b(g, u, c):
        o0 = u * 1024 + (c % WG) * 128
        return bv[g % 2][:, o0:o0 + 64]

    def sl_v(g, u, c):
        o0 = u * 1024 + (c % WG) * 128 + 64
        return bv[g % 2][:, o0:o0 + 64]

    def sl_bpp(c, u):
        return btn[:, c * UPC + u: c * UPC + u + 1]

    def usl(t, u, w=C):
        return t[:, u * w:(u + 1) * w]

    U = range(UPC)
    pingpong = [p_c0, p_c1]
    lvl = [0]

    def pe_level(body, extra=()):
        bank = pingpong[lvl[0] % 2]
        w = [("s_v", p.ct["s_v"])] + list(extra)
        r = p.seg("tensor", lambda te: body(te, bank), waits=w)
        lvl[0] += 1
        return r, bank

    for c in range(n_chunks):
        g = c // WG
        if (c % WG) == 0 and g + 1 < NG:
            load_group(g + 1)
        gw = [("s_w%d" % (g % 2), gload[g])] if (c % WG) == 0 else []
        par = c % 2

        r, bk = pe_level(lambda te, bk: [te.matmul(bk[:, 64 * u:64 * u + 64], sl_kt(g, u, c),
                                                   sl_kt(g, u, c), start=True, stop=True) for u in U],
                         extra=gw + ([("s_v", init_v)] if c == 0 else []))
        p.seg("vector", lambda v, bk=bk, c=c: [v.scalar_tensor_tensor(
            usl(S_, u), bk[:, 64 * u:64 * u + 64], sl_bpp(c, u), mask[:, :], ALU.mult, ALU.mult)
            for u in U], waits=[("s_pe", r)] + ([("s_ld", cl), ("s_v", init_v)] if c == 0 else []))
        r, bk = pe_level(lambda te, bk: [te.matmul(bk[:, 64 * u:64 * u + 64], sl_kt(g, u, c),
                                                   sl_qt(g, u, c), start=True, stop=True) for u in U])
        p.seg("vector", lambda v, bk=bk, c=c, g=g: (
            v.tensor_tensor(tqk[:, 0:512], bk[:, 0:512],
                            tmt8[g % 2][:, (c % WG) * 512:(c % WG) * 512 + 512], ALU.mult),
            v.tensor_scalar(kw[:, :].rearrange("p (u d) -> p u d", u=8),
                            km[g % 2][:, :].rearrange("p (u w) -> p u w", u=8)
                            [:, :, (c % WG) * 64:(c % WG) * 64 + 64],
                            wtab[:, c:c + 1], None, ALU.mult)), waits=[("s_pe", r)])
        r = p.seg("tensor", lambda te: [te.transpose(p_tr[:, 64 * u:64 * u + 64], usl(S_, u),
                                                     id64[:, :]) for u in U],
                  waits=[("s_v", p.ct["s_v"])])
        p.seg("vector", lambda v: v.tensor_copy(At[0][:, 0:512], p_tr[:, 0:512]),
              waits=[("s_pe", r)])
        for j in range(6):
            src_b = (lambda u: sl_b(g, u, c)) if j == 0 else (lambda u: usl(XB, u))
            src_v = (lambda u: sl_v(g, u, c)) if j == 0 else (lambda u: usl(XV, u))
            r, bk = pe_level(lambda te, bk, j=j, src_b=src_b: [
                te.matmul(bk[:, 64 * u:64 * u + 64], usl(At[j], u), src_b(u), start=True, stop=True)
                for u in U])
            if j == 0:
                p.seg("vector", lambda v, bk=bk, c=c, g=g: v.tensor_tensor(
                    XB[:, 0:512].rearrange("p (u d) -> p u d", u=8),
                    bv[g % 2][:, :].rearrange("p (u q) -> p u q", u=8)
                    [:, :, (c % WG) * 128:(c % WG) * 128 + 64],
                    bk[:, 0:512].rearrange("p (u d) -> p u d", u=8), ALU.add),
                    waits=[("s_pe", r)])
            else:
                p.seg("vector", lambda v, bk=bk: v.tensor_tensor(
                    XB[:, 0:512], XB[:, 0:512], bk[:, 0:512], ALU.add),
                    waits=[("s_pe", r)])
            r, bk = pe_level(lambda te, bk, j=j, src_v=src_v: [
                te.matmul(bk[:, 64 * u:64 * u + 64], usl(At[j], u), src_v(u), start=True, stop=True)
                for u in U])
            if j == 0:
                p.seg("vector", lambda v, bk=bk, c=c, g=g: v.tensor_tensor(
                    XV[:, 0:512].rearrange("p (u d) -> p u d", u=8),
                    bv[g % 2][:, :].rearrange("p (u q) -> p u q", u=8)
                    [:, :, (c % WG) * 128 + 64:(c % WG) * 128 + 128],
                    bk[:, 0:512].rearrange("p (u d) -> p u d", u=8), ALU.add),
                    waits=[("s_pe", r)])
            else:
                p.seg("vector", lambda v, bk=bk: v.tensor_tensor(
                    XV[:, 0:512], XV[:, 0:512], bk[:, 0:512], ALU.add),
                    waits=[("s_pe", r)])
            asrc = (lambda u: usl(S_, u)) if j == 0 else (lambda u: usl(Aj[j - 1], u))
            if j < 4:
                r, bk = pe_level(lambda te, bk, j=j, asrc=asrc: [
                    te.matmul(bk[:, 64 * u:64 * u + 64], usl(At[j], u), asrc(u), start=True, stop=True)
                    for u in U])
                p.seg("vector", lambda v, bk=bk, j=j: v.tensor_copy(Aj[j][:, 0:512],
                      bk[:, 0:512]), waits=[("s_pe", r)])
            if j < 5:
                r, bk = pe_level(lambda te, bk, j=j, asrc=asrc: [
                    te.matmul(bk[:, 64 * u:64 * u + 64], asrc(u), usl(At[j], u), start=True, stop=True)
                    for u in U])
                p.seg("vector", lambda v, bk=bk, j=j: v.tensor_copy(At[j + 1][:, 0:512],
                      bk[:, 0:512]), waits=[("s_pe", r)])
        r, bk = pe_level(lambda te, bk: [te.matmul(bk[:, 64 * u:64 * u + 64], usl(XB, u),
                                                   sl_km(g, u, c), start=True, stop=True) for u in U])
        p.seg("vector", lambda v, bk=bk: v.tensor_scalar(
            negat[:, 0:512], bk[:, 0:512], -1.0, None, ALU.mult), waits=[("s_pe", r)])
        r, bk = pe_level(lambda te, bk: [te.matmul(bk[:, 64 * u:64 * u + 64], usl(XB, u),
                                                   usl(kw, u), start=True, stop=True) for u in U])
        p.seg("vector", lambda v, bk=bk, c2v=float(C2_TAB[c]): v.scalar_tensor_tensor(
            gt[:, 0:512], id64r[:, 0:512], c2v, bk[:, 0:512], ALU.mult, ALU.subtract),
            waits=[("s_pe", r)])
        r, bk = pe_level(lambda te, bk: [te.matmul(bk[:, 64 * u:64 * u + 64], usl(XB, u),
                                                   usl(tqk, u), start=True, stop=True) for u in U])
        p.seg("vector", lambda v, bk=bk, c=c, g=g: (
            v.tensor_scalar(negdt[:, 0:512], bk[:, 0:512], -1.0, None, ALU.mult),
            [v.tensor_tensor(usl(qat, u), sl_qt(g, u, c), arep[:, C * c:C * c + C], ALU.mult)
             for u in U],
            [v.tensor_tensor(usl(qct, u), sl_qt(g, u, c), crep[:, C * c:C * c + C], ALU.mult)
             for u in U]), waits=[("s_pe", r)])
        prep_v = p.ct["s_v"]

        def seq_body(te, c=c, g=g):
            r = []
            for u in U:
                r.append(te.matmul(p_W[:, 64 * u:64 * u + 64], sl_km(g, u, c), usl(XV, u),
                          start=(c == 0 and u == 0), stop=False, skip_group_check=True))
            for u in U:
                r.append(te.matmul(p_Wa[:, 64 * u:64 * u + 64], usl(kw, u), usl(XV, u),
                          start=(u == 0), stop=(c == 0), skip_group_check=True))
            for u in U:
                r.append(te.matmul(p_O[:, 64 * u:64 * u + 64], usl(tqk, u), usl(XV, u),
                          start=(u == 0), stop=(c == 0), skip_group_check=True))
            if c > 0:
                for u in U:
                    r.append(te.matmul(p_W[:, 64 * u:64 * u + 64], usl(negat, u), usl(wsb, u),
                              start=False, stop=True, skip_group_check=True))
                for u in U:
                    r.append(te.matmul(p_Wa[:, 64 * u:64 * u + 64], usl(gt, u), usl(wsb, u),
                              start=False, stop=True, skip_group_check=True))
                for u in U:
                    r.append(te.matmul(p_O[:, 64 * u:64 * u + 64], usl(qat, u), usl(wavb, u),
                              start=False, stop=False, skip_group_check=True))
                for u in U:
                    r.append(te.matmul(p_O[:, 64 * u:64 * u + 64], usl(qct, u), usl(wsb, u),
                              start=False, stop=False, skip_group_check=True))
                for u in U:
                    r.append(te.matmul(p_O[:, 64 * u:64 * u + 64], usl(negdt, u), usl(wsb, u),
                              start=False, stop=True, skip_group_check=True))
            return r
        seq = p.seg("tensor", seq_body, waits=[("s_v", prep_v)])

        c1v = float(C1_TAB[c])
        sev = p.seg("vector", lambda v, par=par, c1v=c1v: (
            v.tensor_copy(wsb[:, 0:512], p_W[:, 0:512]),
            v.scalar_tensor_tensor(wavf[par][:, 0:512], wavf[1 - par][:, 0:512], c1v,
                                   p_Wa[:, 0:512], ALU.mult, ALU.add),
            v.drain(),
            v.tensor_copy(wavb[:, 0:512], wavf[par][:, 0:512])),
            waits=[("s_pe", seq)])
        ob = ot[c % 2]
        oev = p.seg("vector", lambda v, ob=ob: (
            v.tensor_copy(ob[:, :], p_O[:, 0:512]),
            v.drain(),
            v.tensor_tensor(sqt[:, :], ob[:, :], ob[:, :], ALU.mult),
            v.drain(),
            v.tensor_reduce(ssq[:, :], sqt[:, :].rearrange("p (u d) -> p u d", d=C), AX.X, ALU.add),
            v.drain(),
            v.tensor_scalar(ssq[:, :], ssq[:, :], 1.0 / C, None, ALU.mult)),
            waits=[("s_pe", seq)])
        osc = p.seg("scalar", lambda s: (
            s.activation(rsq[:, :], ssq[:, :], AF.Ln, bias=p.cst(1e-6, C)),
            s.drain(),
            s.activation(rsq[:, :], rsq[:, :], AF.Exp, scale=-0.5)),
            waits=[("s_v", oev)])
        ofin = p.seg("vector", lambda v, ob=ob: [v.scalar_tensor_tensor(
            usl(ob, u), usl(ob, u), rsq[:, u:u + 1], rmsw[:, :], ALU.mult, ALU.mult) for u in U],
            waits=[("s_sc", osc)])
        p.dma("sync", [(d_o[c], ob[:, :])], "s_st", waits=[("s_v", ofin)])

    if debug:
        dbg_names = ["b_wsb", "b_wavb", "b_negat", "b_gt", "b_negdt", "b_qat", "b_qct",
                     "b_XB", "b_XV", "b_S", "b_tqk", "b_kw"]
        dd = {n: p.dout("dbg_" + n, [C, UPC * C], BF16) for n in dbg_names}
        dv = p.seg("vector", lambda v: [v.tensor_copy(dd[n][:, :] if False else None, None)
                                        for n in []], waits=[("s_v", p.ct["s_v"]), ("s_sc", p.ct["s_sc"])])
        dcp = p.seg("vector", lambda v: [v.tensor_copy(p.t[n][:, :], p.t[n][:, :]) for n in []])
        p.dma("sync", [(dd[n], p.t[n][:, :]) for n in dbg_names], "s_st",
              waits=[("s_v", p.ct["s_v"]), ("s_sc", p.ct["s_sc"]), ("s_pe", p.ct["s_pe"])])
    p.seg("sync", lambda s: s.wait_ge(p.sems["s_st"], p.ct["s_st"]), pub=False)
    if standalone:
        return p.finish()
    return None


def l2_host_inputs(core, kn, qn, bb, vv, beta, rms_w):
    units = [((8 * core + i) // H, (8 * core + i) % H) for i in range(UPC)]

    def head_units(t):
        return np.stack([np.asarray(t[h * DH:(h + 1) * DH, b * L:(b + 1) * L])
                         for (b, h) in units], 0).astype(np.float32)

    knu, qnu = head_units(kn), head_units(qn)
    bbu, vvu = head_units(bb), head_units(vv)
    btu = np.stack([np.asarray(beta[h, b * L:(b + 1) * L]) for (b, h) in units], 0).astype(np.float32)

    def fm_img(t):
        return np.ascontiguousarray(
            t.reshape(8, 64, NG, 512).transpose(2, 1, 0, 3).reshape(NG, 64, UPC * 512))

    def tm_img(t):
        x = t.transpose(2, 0, 1).reshape(NG, WG, 64, UPC, DH)
        return np.ascontiguousarray(x.transpose(0, 2, 3, 1, 4).reshape(NG, 64, UPC * 512))

    xb = bbu.transpose(2, 0, 1).reshape(NG, WG, 64, UPC, 1, DH)
    xv = vvu.transpose(2, 0, 1).reshape(NG, WG, 64, UPC, 1, DH)
    bv = np.concatenate([xb, xv], 4)
    bv_img = np.ascontiguousarray(bv.transpose(0, 2, 3, 1, 4, 5).reshape(NG, 64, UPC * 1024))
    bt_img = np.ascontiguousarray(btu.T.reshape(NCHUNK, C, UPC).transpose(1, 0, 2)
                                  .reshape(C, NCHUNK * UPC))
    return {
        "kf": _bf(fm_img(knu)), "qf": _bf(fm_img(qnu)), "km": _bf(tm_img(knu)),
        "bv": _bf(bv_img), "betat": _f32(bt_img),
        "tmt8": _bf(np.stack([np.concatenate(
            [np.tile(TMAT_TAB[g * WG + cc].T, (1, UPC)) for cc in range(WG)], 1)
            for g in range(NG)], 0)),
        "id64r": _bf(np.tile(IDENT64, (1, UPC))),
        "arep": _bf(np.broadcast_to(A_TAB.reshape(1, NCHUNK * C), (C, NCHUNK * C))),
        "crep": _bf(np.broadcast_to((1.0 - A_TAB).reshape(1, NCHUNK * C), (C, NCHUNK * C))),
        "wtab": _f32(W_TAB.T),
        "mask01": _bf(MASK_SL), "id64": _bf(IDENT64),
        "rmsw": _bf(np.broadcast_to(np.asarray(rms_w).reshape(1, C), (C, C))),
    }


def l2_host_output(core, o_img):
    t = np.asarray(o_img).reshape(NCHUNK, C, UPC, DH)
    return np.ascontiguousarray(t.transpose(2, 0, 1, 3).reshape(UPC, L, DH))


def emit_ln(p, tag, x, W, eps, var_scale=1.0, gcol=None, bcol=None, want_neg=False,
            pre_wait=(), src_bf=False):
    xbf = x if src_bf else p.sbr(f"ln_{tag}_xbf", [128, 8 * W], BF16)
    sq = p.sbr(f"ln_{tag}_sq", [128, 8 * W], BF16)
    xln = p.sbr(f"ln_{tag}_xln", [128, 8 * W], BF16)
    negx = p.sbr(f"ln_{tag}_neg", [128, 8 * W], BF16) if want_neg else None
    srow = p.sbr(f"ln_{tag}_s0", [1, W], F32)
    srow1 = p.sbr(f"ln_{tag}_s1", [1, W], F32)
    mu = p.sbr(f"ln_{tag}_mu", [1, W], F32)
    rs = p.sbr(f"ln_{tag}_rs", [1, W], F32)
    vt = p.sbr(f"ln_{tag}_vt", [1, W], F32)
    onesb = p.t[p.pre + "ones_bf"]
    onesr = p.t[p.pre + "ones_row"]
    p_a = p.psr("p_repA", [128, 512])
    p_b = p.psr("p_repB", [128, 512])
    p_s0 = p.psr("p_stat", [1, 512])
    p_s1 = p.psr("p_stat2", [1, 512])

    if src_bf:
        cast = p.seg("vector", lambda v: None, waits=pre_wait)
    else:
        cast = p.seg("vector", lambda v: [v.tensor_copy(ptile(xbf, j, W), ptile(x, j, W))
                                          for j in range(8)], waits=pre_wait)
    sqd = p.seg("scalar", lambda s: [s.activation(ptile(sq, j, W), ptile(xbf, j, W), AF.Square)
                                     for j in range(8)], waits=[("s_v", cast)])
    sums_done = None
    for (cc, cw) in col_chunks(W):
        mmv = p.seg("tensor", lambda te, cc=cc, cw=cw: (
            [te.matmul(p_s0[0:1, 0:cw], onesb[:, 0:1], ptile(xbf, j, W, cc, cw),
                       start=(j == 0), stop=(j == 7)) for j in range(8)],
            [te.matmul(p_s1[0:1, 0:cw], onesb[:, 0:1], ptile(sq, j, W, cc, cw),
                       start=(j == 0), stop=(j == 7)) for j in range(8)]),
            waits=[("s_sc", sqd), ("s_v", p.ct["s_v"]), ("s_ld", p.ct["s_ld"])])
        sums_done = p.seg("vector", lambda v, cc=cc, cw=cw: (
            v.tensor_copy(srow[0:1, cc:cc + cw], p_s0[0:1, 0:cw]),
            v.tensor_copy(srow1[0:1, cc:cc + cw], p_s1[0:1, 0:cw])),
            waits=[("s_pe", mmv)])
    vard = p.seg("vector", lambda v: (
        v.tensor_scalar(mu[0:1, :], srow[0:1, :], 1.0 / D, None, ALU.mult),
        v.tensor_scalar(vt[0:1, :], srow1[0:1, :], 1.0 / D, None, ALU.mult),
        v.drain(),
        v.tensor_tensor(srow[0:1, :], mu[0:1, :], mu[0:1, :], ALU.mult),
        v.drain(),
        v.tensor_tensor(vt[0:1, :], vt[0:1, :], srow[0:1, :], ALU.subtract)),
        waits=[("s_v", sums_done)])
    rsd = p.seg("scalar", lambda s: (
        s.activation(rs[0:1, :], vt[0:1, :], AF.Ln, bias=p.cst(eps, 1), scale=float(var_scale)),
        s.drain(),
        s.activation(rs[0:1, :], rs[0:1, :], AF.Exp, scale=-0.5)),
        waits=[("s_v", vard)])
    done = None
    for (cc, cw) in col_chunks(W):
        repd = p.seg("tensor", lambda te, cc=cc, cw=cw: (
            te.matmul(p_a[:, 0:cw], onesr[0:1, :], mu[0:1, cc:cc + cw], start=True, stop=True),
            te.matmul(p_b[:, 0:cw], onesr[0:1, :], rs[0:1, cc:cc + cw], start=True, stop=True)),
            waits=[("s_sc", rsd), ("s_v", p.ct["s_v"])])

        def apply(v, cc=cc, cw=cw):
            for j in range(8):
                v.tensor_tensor(ptile(xln, j, W, cc, cw), ptile(xbf, j, W, cc, cw),
                                p_a[:, 0:cw], ALU.subtract)
            v.drain()
            for j in range(8):
                v.tensor_tensor(ptile(xln, j, W, cc, cw), ptile(xln, j, W, cc, cw),
                                p_b[:, 0:cw], ALU.mult)
            if gcol is not None:
                v.drain()
                for j in range(8):
                    v.tensor_scalar(ptile(xln, j, W, cc, cw), ptile(xln, j, W, cc, cw),
                                    gcol[:, j:j + 1], bcol[:, j:j + 1], ALU.mult, ALU.add)
            if want_neg:
                v.drain()
                for j in range(8):
                    v.tensor_scalar(ptile(negx, j, W, cc, cw), ptile(xln, j, W, cc, cw),
                                    -1.0, None, ALU.mult)
        done = p.seg("vector", apply, waits=[("s_pe", repd)])
    return xln, negx, done


def ln_common(p):
    p.sb("ones_bf", [128, 1], BF16)
    p.sb("ones_row", [1, 128], F32)
    ob = p.seg("vector", lambda v: (v.memset(p.t[p.pre + "ones_bf"][:, :], 1.0),
                                    v.memset(p.t[p.pre + "ones_row"][:, :], 1.0)))
    return ob


def build_L3():
    p = Prog("L3")
    d_x = p.din("x", [D, TOK], F32)
    d_of = p.din("of", [D, TOK], BF16)
    d_wo = p.din("woT", [D, D], F32)
    d_bo = p.din("bocol", [128, 8], F32)
    d_x1 = p.dout("x1", [D, TOK], F32)

    x = p.sb("b_x", [128, 8 * TOK], F32)
    of = p.sb("b_of", [128, 8 * TOK], BF16)
    wo = p.sb("b_wo", [128, 8 * D], BF16)
    bo = p.sb("b_bo", [128, 8], F32)
    p_mm = p.ps("p_mm", [128, 512])
    p_mm2 = p.ps("p_mm2", [128, 512])
    p.sem("s_ld", "s_g", "s_pe", "s_v", "s_sc", "s_st")
    p.begin()

    wl = p.dma("gpsimd", [(wo[:, :].rearrange("p (j o) -> p j o", j=8),
                           d_wo.rearrange("(j p) o -> p j o", p=128))], "s_g")
    ll = p.dma("sync", [(x[:, :].rearrange("p (j w) -> p j w", j=8),
                         d_x.rearrange("(j p) w -> p j w", p=128)),
                        (of[:, :].rearrange("p (j w) -> p j w", j=8),
                         d_of.rearrange("(j p) w -> p j w", p=128)),
                        (bo[:, :], d_bo)], "s_ld")
    hist = []
    for ot in range(8):
        for ci, (cc, cw) in enumerate(col_chunks(TOK)):
            pmm = p_mm if len(hist) % 2 == 0 else p_mm2
            wts = [("s_ld", ll), ("s_g", wl)] if not hist else []
            if len(hist) >= 2:
                wts = wts + [("s_v", hist[-2])]
            mmd = p.seg("tensor", lambda te, ot=ot, cc=cc, cw=cw, pmm=pmm: [
                te.matmul(pmm[:, 0:cw], wo[:, (j * 8 + ot) * 128:(j * 8 + ot + 1) * 128],
                          ptile(of, j, TOK, cc, cw), start=(j == 0), stop=(j == 7))
                for j in range(8)], waits=wts)
            ev = p.seg("vector", lambda v, ot=ot, cc=cc, cw=cw, pmm=pmm: v.scalar_tensor_tensor(
                ptile(x, ot, TOK, cc, cw), pmm[:, 0:cw], bo[:, ot:ot + 1],
                ptile(x, ot, TOK, cc, cw), ALU.add, ALU.add),
                waits=[("s_pe", mmd)])
            hist.append(ev)
    st = p.dma("sync", [(d_x1.rearrange("(j p) w -> p j w", p=128),
                         x[:, :].rearrange("p (j w) -> p j w", j=8))], "s_st",
               waits=[("s_v", hist[-1])])
    p.seg("sync", lambda s: s.wait_ge(p.sems["s_st"], st), pub=False)
    return p.finish()


def build_L4(debug=False, p=None, io=None, tag=""):
    standalone = p is None
    if standalone:
        p = Prog("L4")
    TB = 512
    NTB = TOK // TB
    d_x = dio(p, io, "x", [D, TOK], F32)
    d_k0t = dio(p, io, "k0t", [D, MLPD], U8)
    d_k0b = dio(p, io, "k0b", [D, MLPD], U8)
    d_k1 = dio(p, io, "k1", [MLPD, MLPD], U8)
    d_k2 = dio(p, io, "k2", [MLPD, D], U8)
    d_s = dio(p, io, "svals", [1, 3], F32)
    d_xo = dio(p, io, "xo", [D, TOK], F32, out=True)

    x = p.sb("b_x", [128, 8 * TB], F32)
    h1 = p.sb("b_h1", [128, 32 * TB], BF16)
    h2 = p.sb("b_h2", [128, 32 * TB], BF16)
    w0 = [p.sb("b_w0%d" % i, [128, 2048], BF16) for i in range(2)]
    w1 = [p.sb("b_w1%d" % i, [128, 32 * 128], BF16) for i in range(2)]
    w2 = [p.sb("b_w2%d" % i, [128, 32 * 128], BF16) for i in range(2)]
    sv = p.sb("b_sv", [1, 3], F32)
    scol = p.sb("b_scol", [128, 3], F32)
    onesrb = p.sb("b_onesrb", [1, 128], BF16)
    nr1 = p.sb("b_nr1", [1, 512], F32)
    nr2 = p.sb("b_nr2", [1, 512], F32)
    p_mm = p.ps("p_mm", [128, 512])
    p_mm2 = p.ps("p_mm2", [128, 512])
    p_sc = p.ps("p_sc", [128, 3])
    if standalone:
        p.sem("s_ld", "s_g0", "s_g1", "s_pe", "s_v", "s_sc", "s_st")
        p.begin(consts=[1e-5])
    ob = ln_common(p)
    orb = p.seg("vector", lambda v: v.memset(onesrb[:, :], 1.0))

    sl = p.dma("sync", [(sv[:, :], d_s)], "s_ld")
    scd = p.seg("tensor", lambda te: te.matmul(p_sc[:, 0:3], p.t[p.pre + "ones_row"][0:1, :], sv[0:1, :],
                                               start=True, stop=True),
                waits=[("s_ld", p.ct["s_ld"]), ("s_v", ob)])
    scv = p.seg("vector", lambda v: (
        v.tensor_scalar(scol[:, 0:1], p_sc[:, 0:1], 2.0, None, ALU.mult),
        v.tensor_scalar(scol[:, 1:3], p_sc[:, 1:3], 4.0, None, ALU.mult)),
        waits=[("s_pe", scd)])

    def load_w0(mb, tb, waits=()):
        i = mb % 2
        return p.dma("gpsimd", [
            (w0[i][:, 0:1024].rearrange("p (j o) -> p j o", j=8),
             d_k0t[:, mb * 128:(mb + 1) * 128].rearrange("(j p) o -> p j o", p=128)),
            (w0[i][:, 1024:2048].rearrange("p (j o) -> p j o", j=8),
             d_k0b[:, mb * 128:(mb + 1) * 128].rearrange("(j p) o -> p j o", p=128))],
            "s_g%d" % i, waits=list(waits))

    def load_w1(mb, tb, waits=()):
        i = mb % 2
        return p.dma("gpsimd", [
            (w1[i][:, :].rearrange("p (j o) -> p j o", j=32),
             d_k1[:, mb * 128:(mb + 1) * 128].rearrange("(j p) o -> p j o", p=128))],
            "s_g%d" % i, waits=list(waits))

    def load_w2(mb, tb, waits=()):
        i = mb % 2
        return p.dma("gpsimd", [
            (w2[i][:, :].rearrange("p (j o) -> p j o", j=32),
             d_k2[:, mb * 128:(mb + 1) * 128].rearrange("(j p) o -> p j o", p=128))],
            "s_g%d" % i, waits=list(waits))

    for tb in range(NTB):
        lx = p.dma("sync", [(x[:, :].rearrange("p (j w) -> p j w", j=8),
                             d_x[:, tb * TB:(tb + 1) * TB].rearrange("(j p) w -> p j w", p=128))],
                   "s_ld", waits=[("s_v", p.ct["s_v"]), ("s_sc", p.ct["s_sc"])] if tb else ())
        xln, negx, lnd = emit_ln(p, "a", x, TB, 1e-5, want_neg=True,
                                 pre_wait=[("s_ld", lx)])
        lnd = p.seg("vector", lambda v: (
            [v.tensor_scalar(ptile(xln, j, TB), ptile(xln, j, TB), scol[:, 0:1], None, ALU.mult)
             for j in range(8)],
            [v.tensor_scalar(ptile(negx, j, TB), ptile(negx, j, TB), scol[:, 0:1], None, ALU.mult)
             for j in range(8)]), waits=[("s_v", lnd), ("s_v", scv)])

        w0w = [("s_pe", p.ct["s_pe"])] if tb > 0 else []
        wl_pend = {0: load_w0(0, tb, w0w), 1: load_w0(1, tb, w0w)}
        pe_b1 = p.ct["s_pe"]
        sc_b1 = p.ct["s_sc"]

        def mega1(s, pe_b1=pe_b1):
            for mb in range(32):
                s.wait_ge(p.sems["s_pe"], pe_b1 + mb + 1)
                s.activation(h1[:, mb * TB:(mb + 1) * TB],
                             (p_mm if mb % 2 == 0 else p_mm2)[:, 0:TB],
                             AF.Relu).then_inc(p.sems["s_sc"], 1)
        p.block.scalar(mega1)
        p.ct["s_sc"] += 32
        for mb in range(32):
            pmm = p_mm if mb % 2 == 0 else p_mm2
            wts = [("s_g%d" % (mb % 2), wl_pend[mb])]
            if mb == 0:
                wts += [("s_v", lnd), ("s_v", scv)]
            if mb >= 2:
                wts += [("s_sc", sc_b1 + mb - 1)]
            i = mb % 2
            mmd = p.seg("tensor", lambda te, i=i, pmm=pmm: (
                [te.matmul(pmm[:, 0:TB], w0[i][:, j * 128:(j + 1) * 128],
                           ptile(xln, j, TB), start=(j == 0), stop=False) for j in range(8)],
                [te.matmul(pmm[:, 0:TB], w0[i][:, 1024 + j * 128:1024 + (j + 1) * 128],
                           ptile(negx, j, TB), start=False, stop=(j == 7)) for j in range(8)]),
                waits=wts)
            if mb + 2 < 32:
                wl_pend[mb + 2] = load_w0(mb + 2, tb, [("s_pe", mmd)])
        assert p.ct["s_pe"] == pe_b1 + 32
        l0d = sc_b1 + 32
        h1s = p.seg("vector", lambda v: [v.tensor_scalar(
            h1[:, mb * TB:(mb + 1) * TB], h1[:, mb * TB:(mb + 1) * TB], scol[:, 1:2],
            None, ALU.mult) for mb in range(32)], waits=[("s_sc", l0d)])

        rs1m = p.seg("tensor", lambda te: [te.matmul(p.t[p.pre + "p_stat"][0:1, 0:TB],
                     p.t[p.pre + "ones_bf"][:, 0:1], h1[:, j * TB:(j + 1) * TB],
                     start=(j == 0), stop=(j == 31)) for j in range(32)],
                     waits=[("s_v", h1s)])
        rs1v = p.seg("vector", lambda v: v.tensor_scalar(nr1[0:1, 0:TB],
                     p.t[p.pre + "p_stat"][0:1, 0:TB], -0.5, None, ALU.mult),
                     waits=[("s_pe", rs1m)])

        w1w = [("s_pe", p.ct["s_pe"])]
        wl_pend = {0: load_w1(0, tb, w1w), 1: load_w1(1, tb, w1w)}
        pe_b2 = p.ct["s_pe"]
        sc_b2 = p.ct["s_sc"]

        def mega2(s, pe_b2=pe_b2):
            for mb in range(32):
                s.wait_ge(p.sems["s_pe"], pe_b2 + mb + 1)
                s.activation(h2[:, mb * TB:(mb + 1) * TB],
                             (p_mm if mb % 2 == 0 else p_mm2)[:, 0:TB],
                             AF.Relu).then_inc(p.sems["s_sc"], 1)
        p.block.scalar(mega2)
        p.ct["s_sc"] += 32
        for mb in range(32):
            pmm = p_mm if mb % 2 == 0 else p_mm2
            wts = [("s_g%d" % (mb % 2), wl_pend[mb])]
            if mb == 0:
                wts += [("s_v", h1s), ("s_v", rs1v)]
            if mb >= 2:
                wts += [("s_sc", sc_b2 + mb - 1)]
            i = mb % 2
            mmd = p.seg("tensor", lambda te, i=i, pmm=pmm: (
                [te.matmul(pmm[:, 0:TB], w1[i][:, j * 128:(j + 1) * 128],
                           h1[:, j * TB:(j + 1) * TB], start=(j == 0), stop=False)
                 for j in range(32)],
                te.matmul(pmm[:, 0:TB], p.t[p.pre + "ones_row"][0:1, :], nr1[0:1, 0:TB],
                          start=False, stop=True)), waits=wts)
            if mb + 2 < 32:
                wl_pend[mb + 2] = load_w1(mb + 2, tb, [("s_pe", mmd)])
        assert p.ct["s_pe"] == pe_b2 + 32
        l1d = sc_b2 + 32

        rs2m = p.seg("tensor", lambda te: [te.matmul(p.t[p.pre + "p_stat"][0:1, 0:TB],
                     p.t[p.pre + "ones_bf"][:, 0:1], h2[:, j * TB:(j + 1) * TB],
                     start=(j == 0), stop=(j == 31)) for j in range(32)],
                     waits=[("s_sc", l1d), ("s_v", p.ct["s_v"])])
        rs2v = p.seg("vector", lambda v: v.tensor_scalar(nr2[0:1, 0:TB],
                     p.t[p.pre + "p_stat"][0:1, 0:TB], -0.5, None, ALU.mult),
                     waits=[("s_pe", rs2m)])

        w2w = [("s_pe", p.ct["s_pe"])]
        wl_pend = {0: load_w2(0, tb, w2w), 1: load_w2(1, tb, w2w)}
        hist3 = []
        for mb in range(8):
            pmm = p_mm if len(hist3) % 2 == 0 else p_mm2
            wts = [("s_g%d" % (mb % 2), wl_pend[mb])]
            if not hist3:
                wts += [("s_sc", l1d), ("s_v", rs2v)]
            if len(hist3) >= 2:
                wts += [("s_v", hist3[-2])]
            i = mb % 2
            mmd = p.seg("tensor", lambda te, i=i, pmm=pmm: (
                [te.matmul(pmm[:, 0:TB], w2[i][:, j * 128:(j + 1) * 128],
                           h2[:, j * TB:(j + 1) * TB], start=(j == 0), stop=False)
                 for j in range(32)],
                te.matmul(pmm[:, 0:TB], p.t[p.pre + "ones_row"][0:1, :], nr2[0:1, 0:TB],
                          start=False, stop=True)), waits=wts)
            ev = p.seg("vector", lambda v, mb=mb, pmm=pmm: v.scalar_tensor_tensor(
                ptile(x, mb, TB), pmm[:, 0:TB], scol[:, 2:3], ptile(x, mb, TB),
                ALU.mult, ALU.add), waits=[("s_pe", mmd)])
            hist3.append(ev)
            if mb + 2 < 8:
                wl_pend[mb + 2] = load_w2(mb + 2, tb, [("s_pe", mmd)])

        stv = p.dma("sync", [(d_xo[:, tb * TB:(tb + 1) * TB].rearrange("(j p) w -> p j w", p=128),
                              x[:, :].rearrange("p (j w) -> p j w", j=8))], "s_st",
                    waits=[("s_v", hist3[-1])])
        if debug and tb == 0:
            dh1 = p.dout("dbg_h1", [128, 32 * TB], BF16)
            dh2 = p.dout("dbg_h2", [128, 32 * TB], BF16)
            dn1 = p.dout("dbg_nr1", [1, TB], F32)
            dn2 = p.dout("dbg_nr2", [1, TB], F32)
            dxl = p.dout("dbg_xln", [128, 8 * TB], BF16)
            p.dma("sync", [(dh1, h1[:, :]), (dh2, h2[:, :]), (dn1, nr1[0:1, 0:TB]),
                           (dn2, nr2[0:1, 0:TB]), (dxl, p.t[p.pre + "ln_a_xln"][:, :])], "s_st",
                  waits=[("s_v", hist3[-1]), ("s_sc", p.ct["s_sc"])])
        p.barrier("s_st", p.ct["s_st"])
    if standalone:
        return p.finish()
    return None


def build_L5(p=None, io=None):
    standalone = p is None
    if standalone:
        p = Prog("L5")
    TB = 512
    NTB = TOK // TB
    d_x = dio(p, io, "x", [D, TOK], F32)
    d_wq = dio(p, io, "wq", [D, D], F32)
    d_wk = dio(p, io, "wk", [D, D], F32)
    d_wv = dio(p, io, "wv", [D, D], F32)
    d_whq = dio(p, io, "whq", [128, 8 * NB], F32)
    d_whk = dio(p, io, "whk", [128, 8 * NB], F32)
    d_g = dio(p, io, "gcol", [128, 8], F32)
    d_b = dio(p, io, "bcol", [128, 8], F32)
    d_qf = dio(p, io, "qf", [D, TOK], BF16, out=True)
    d_kf = dio(p, io, "kf", [D, TOK], BF16, out=True)
    d_vf = dio(p, io, "vf", [D, TOK], BF16, out=True)
    d_qbu = dio(p, io, "qbu", [TOK // 512, 128, 4 * 512], BF16, out=True)
    d_kbu = dio(p, io, "kbu", [TOK // 512, 128, 4 * 512], BF16, out=True)

    x = p.sb("b_x", [128, 8 * TB], F32)
    wq = p.sb("b_wq", [128, 8 * D], BF16)
    wk = p.sb("b_wk", [128, 8 * D], BF16)
    wv = p.sb("b_wv", [128, 8 * D], BF16)
    whq = p.sb("b_whq", [128, 8 * NB], BF16)
    whk = p.sb("b_whk", [128, 8 * NB], BF16)
    gcol = p.sb("b_g", [128, 8], F32)
    bcol = p.sb("b_b", [128, 8], F32)
    qf = p.sb("b_qf", [128, 8 * TB], BF16)
    kf = p.sb("b_kf", [128, 8 * TB], BF16)
    vf = p.sb("b_vf", [128, 8 * TB], BF16)
    qbu = p.sb("b_qbu", [128, 4 * TB], BF16)
    kbu = p.sb("b_kbu", [128, 4 * TB], BF16)
    rrow = p.sb("b_rrow", [1, 512], F32)
    p_mm = p.ps("p_mm", [128, 512])
    p_mm2 = p.ps("p_mm2", [128, 512])
    p_qb = p.ps("p_qb", [128, 512])
    p_rep = p.ps("p_rep", [128, 512])
    if standalone:
        p.sem("s_ld", "s_g", "s_pe", "s_v", "s_sc", "s_st")
        p.begin(consts=[1e-11])
    ob = ln_common(p)

    wl = p.dma("gpsimd", [
        (wq[:, :].rearrange("p (j o) -> p j o", j=8), d_wq.rearrange("(j p) o -> p j o", p=128)),
        (wk[:, :].rearrange("p (j o) -> p j o", j=8), d_wk.rearrange("(j p) o -> p j o", p=128)),
        (wv[:, :].rearrange("p (j o) -> p j o", j=8), d_wv.rearrange("(j p) o -> p j o", p=128)),
        (whq[:, :], d_whq), (whk[:, :], d_whk)], "s_g")
    cl = p.dma("sync", [(gcol[:, :], d_g), (bcol[:, :], d_b)], "s_ld")

    for tb in range(NTB):
        lx = p.dma("sync", [(x[:, :].rearrange("p (j w) -> p j w", j=8),
                             d_x[:, tb * TB:(tb + 1) * TB].rearrange("(j p) w -> p j w", p=128))],
                   "s_ld", waits=[("s_v", p.ct["s_v"]), ("s_sc", p.ct["s_sc"])] if tb else ())
        zln, _, lnd = emit_ln(p, "a", x, TB, 1e-11, var_scale=1.0 + 1e-6, gcol=gcol, bcol=bcol,
                              pre_wait=[("s_ld", lx)])

        hist = []
        for pi, (wsb, dst, sc) in enumerate([(wq, qf, 0.125), (wk, kf, 1.0), (wv, vf, 1.0)]):
            for ot in range(8):
                pmm = p_mm if len(hist) % 2 == 0 else p_mm2
                wts = [("s_v", lnd), ("s_g", wl)] if not hist else []
                if len(hist) >= 2:
                    wts = wts + [("s_sc", hist[-2])]
                mmd = p.seg("tensor", lambda te, wsb=wsb, ot=ot, pmm=pmm: [
                    te.matmul(pmm[:, 0:TB], wsb[:, (j * 8 + ot) * 128:(j * 8 + ot + 1) * 128],
                              ptile(zln, j, TB), start=(j == 0), stop=(j == 7))
                    for j in range(8)], waits=wts)
                ev = p.seg("scalar", lambda s, dst=dst, ot=ot, sc=sc, pmm=pmm:
                           s.activation(ptile(dst, ot, TB), pmm[:, 0:TB], AF.Identity,
                                        scale=float(sc)), waits=[("s_pe", mmd)])
                hist.append(ev)
        projd = hist[-1]

        for hh in range(H):
            po = (hh % 2) * 64
            qo = (hh % 4) * 32
            jt = hh // 2
            cb = hh // 4
            for src_, wh, dst, scl, e0 in ((qf, whq, qbu, 0.1, 8.0), (kf, whk, kbu, 1.0, 1.0)):
                lg = p.seg("tensor", lambda te, src_=src_, wh=wh, po=po, qo=qo, jt=jt: te.matmul(
                    p_qb[qo:qo + 32, 0:TB], wh[po:po + 64, jt * NB:jt * NB + NB],
                    src_[po:po + 64, jt * TB:jt * TB + TB],
                    start=True, stop=True, tile_position=(po, qo)),
                    waits=[("s_sc", projd), ("s_v", p.ct["s_v"])])
                ex = p.seg("scalar", lambda s, dst=dst, qo=qo, cb=cb, e0=e0: s.activation(
                    dst[qo:qo + 32, cb * TB:cb * TB + TB], p_qb[qo:qo + 32, 0:TB],
                    AF.Exp, scale=float(e0)), waits=[("s_pe", lg)])
                sm = p.seg("tensor", lambda te, dst=dst, qo=qo, cb=cb: te.matmul(
                    p_qb[0:1, 0:TB], p.t[p.pre + "ones_bf"][qo:qo + 32, 0:1],
                    dst[qo:qo + 32, cb * TB:cb * TB + TB],
                    start=True, stop=True, tile_position=(qo, 0)),
                    waits=[("s_sc", ex)])
                rc = p.seg("vector", lambda v: v.reciprocal(rrow[0:1, 0:TB], p_qb[0:1, 0:TB]),
                           waits=[("s_pe", sm)])
                rp = p.seg("tensor", lambda te: te.matmul(
                    p_rep[:, 0:TB], p.t[p.pre + "ones_row"][0:1, :], rrow[0:1, 0:TB], start=True, stop=True),
                    waits=[("s_v", rc)])
                p.seg("vector", lambda v, dst=dst, qo=qo, cb=cb, scl=scl: v.scalar_tensor_tensor(
                    dst[qo:qo + 32, cb * TB:cb * TB + TB], dst[qo:qo + 32, cb * TB:cb * TB + TB],
                    float(scl), p_rep[qo:qo + 32, 0:TB], ALU.mult, ALU.mult),
                    waits=[("s_pe", rp)])

        st = p.dma("sync", [
            (d_qf[:, tb * TB:(tb + 1) * TB].rearrange("(j p) w -> p j w", p=128),
             qf[:, :].rearrange("p (j w) -> p j w", j=8)),
            (d_kf[:, tb * TB:(tb + 1) * TB].rearrange("(j p) w -> p j w", p=128),
             kf[:, :].rearrange("p (j w) -> p j w", j=8)),
            (d_vf[:, tb * TB:(tb + 1) * TB].rearrange("(j p) w -> p j w", p=128),
             vf[:, :].rearrange("p (j w) -> p j w", j=8)),
            (d_qbu[tb], qbu[:, :]),
            (d_kbu[tb], kbu[:, :])], "s_st",
            waits=[("s_v", p.ct["s_v"]), ("s_sc", p.ct["s_sc"])])
        p.barrier("s_st", st)
    if standalone:
        return p.finish()
    return None


def build_L6(debug=False, p=None, io=None):
    standalone = p is None
    if standalone:
        p = Prog("L6")
    d_qc = dio(p, io, "qc", [UPC, 96, L], BF16)
    d_kc = dio(p, io, "kc", [UPC, 96, L], BF16)
    d_vc = dio(p, io, "vc", [UPC, 128, 32 * 65], BF16)
    d_o = dio(p, io, "o", [UPC, C, L], BF16, out=True)

    qc = [p.sb("b_qc%d" % i, [96, L], BF16) for i in range(2)]
    kc = [p.sb("b_kc%d" % i, [96, L], BF16) for i in range(2)]
    vc = [p.sb("b_vc%d" % i, [128, 32 * 65], BF16) for i in range(2)]
    sqk = p.sb("b_sqk", [96, L], BF16)
    n2k = p.sb("b_n2k", [1, L], F32)
    n2q = p.sb("b_n2q", [1, L], F32)
    mq2 = p.sb("b_mq2", [1, 2], F32)
    bkr = p.sb("b_bkr", [1, L], F32)
    negbk = p.sb("b_negbk", [128, 32], F32)
    ebc = p.sb("b_ebc", [128, 32], F32)
    psb = p.sb("b_psb", [128, 1024], BF16)
    repb = p.sb("b_repb", [64, 1024], BF16)
    ou = p.sb("b_ou", [C, L], BF16)
    rrow = p.sb("b_rrow", [1, 1024], F32)
    p_s = [p.ps("p_s%d" % i, [128, 1024]) for i in range(2)]
    p_pv = p.ps("p_pv", [65, 1024])
    p_rep = p.ps("p_rep", [64, 1024])
    d_bkt = p.dtmp("bk_bounce", [1, 4096], F32)
    d_id32 = dio(p, io, "id32f", [32, 32], F32)
    if standalone:
        p.sem("s_ld", "s_g", "s_pe", "s_v", "s_sc", "s_st", "s_w0", "s_w1", "s_bk")
        p.begin(consts=[1e-30])
    id32 = p.sb("b_id32", [32, 32], F32)
    b32 = p.sb("b_b32", [32, 128], F32)
    ob = ln_common(p)
    p.dma("sync", [(id32[:, :], d_id32)], "s_ld")
    p_row = p_rep

    uload = {}

    def load_unit(u):
        i = u % 2
        uload[u] = p.dma("sync", [(qc[i][:, :], d_qc[u]), (kc[i][:, :], d_kc[u]),
                                  (vc[i][:, :], d_vc[u])], "s_w%d" % i,
                         waits=[("s_pe", p.ct["s_pe"]), ("s_sc", p.ct["s_sc"]),
                                ("s_v", p.ct["s_v"])] if u >= 2 else ())

    load_unit(0)
    load_unit(1)

    for u in range(UPC):
        i = u % 2
        uw = [("s_w%d" % i, uload[u])]
        sq1 = p.seg("scalar", lambda s, i=i: [s.activation(sqk[:, cc:cc + 512], kc[i][:, cc:cc + 512],
                                                           AF.Square) for cc in range(0, L, 512)],
                    waits=uw + [("s_v", p.ct["s_v"])])
        nk = None
        for ci, cc in enumerate(range(0, L, 512)):
            m1 = p.seg("tensor", lambda te, cc=cc: te.matmul(
                p_row[0:1, 0:512], p.t[p.pre + "ones_bf"][0:96, 0:1], sqk[:, cc:cc + 512],
                start=True, stop=True), waits=[("s_sc", sq1), ("s_v", p.ct["s_v"])])
            nk = p.seg("vector", lambda v, cc=cc: v.tensor_copy(n2k[0:1, cc:cc + 512],
                                                                p_row[0:1, 0:512]),
                       waits=[("s_pe", m1)])
        sq2 = p.seg("scalar", lambda s, i=i: [s.activation(sqk[:, cc:cc + 512], qc[i][:, cc:cc + 512],
                                                           AF.Square) for cc in range(0, L, 512)],
                    waits=uw + [("s_v", nk)])
        nq = None
        for ci, cc in enumerate(range(0, L, 512)):
            m1 = p.seg("tensor", lambda te, cc=cc: te.matmul(
                p_row[0:1, 0:512], p.t[p.pre + "ones_bf"][0:96, 0:1], sqk[:, cc:cc + 512],
                start=True, stop=True), waits=[("s_sc", sq2), ("s_v", p.ct["s_v"])])
            nq = p.seg("vector", lambda v, cc=cc: v.tensor_copy(n2q[0:1, cc:cc + 512],
                                                                p_row[0:1, 0:512]),
                       waits=[("s_pe", m1)])
        mx = p.seg("vector", lambda v: v.reduce_max(mq2[0:1, 0:1], n2q[0:1, :], AX.X, None),
                   waits=[("s_v", nq)])
        lnm = p.seg("scalar", lambda s: (
            s.activation(mq2[0:1, 1:2], mq2[0:1, 0:1], AF.Ln, bias=p.cst(1e-30, 1)),
            s.drain(),
            s.activation(bkr[0:1, :], n2k[0:1, :], AF.Ln, bias=p.cst(1e-30, 1))),
            waits=[("s_v", mx)])
        hb = p.seg("vector", lambda v: v.tensor_scalar(mq2[0:1, 1:2], mq2[0:1, 1:2], 0.5,
                                                       None, ALU.mult), waits=[("s_sc", lnm)])
        bk1 = p.seg("scalar", lambda s: s.activation(bkr[0:1, :], bkr[0:1, :], AF.Exp,
                                                     scale=0.5, bias=mq2[0:1, 1:2]),
                    waits=[("s_v", hb)])
        bs1 = p.dma("sync", [(d_bkt, bkr[0:1, :])], "s_bk",
                    waits=[("s_sc", bk1)], single_packet=True)
        bs2 = p.dma("sync", [(b32[:, :], d_bkt.rearrange("a (k q) -> (a k) q", k=32))], "s_bk")
        trd = p.seg("tensor", lambda te: te.transpose(p_s[0][:, 0:32], b32[:, :], id32[:, :]),
                    waits=[("s_bk", bs2), ("s_ld", p.ct["s_ld"]), ("s_sc", p.ct["s_sc"])])
        ngd = p.seg("vector", lambda v: v.tensor_scalar(negbk[:, :], p_s[0][:, 0:32], -1.0, None,
                                                        ALU.mult), waits=[("s_pe", trd)])
        ebv0 = p.seg("scalar", lambda s, i=i: (
            s.activation(ebc[:, :], negbk[:, :], AF.Exp, scale=-1.0),
            s.drain(),
            [s.copy(vc[i][:, kt * 65 + 64:kt * 65 + 65], ebc[:, kt:kt + 1]) for kt in range(32)]),
            waits=[("s_v", ngd), ("s_w%d" % i, uload[u])])
        ebv = None
        vsc = p.seg("vector", lambda v, i=i: [v.tensor_scalar(
            vc[i][:, kt * 65:kt * 65 + 64], vc[i][:, kt * 65:kt * 65 + 64],
            ebc[:, kt:kt + 1], None, ALU.mult) for kt in range(32)],
            waits=[("s_sc", ebv0)])
        ebv = ebv0

        for qb in range(4):
            q0 = qb * 1024
            pvw = []
            for kt in range(32):
                ps = p_s[kt % 2]
                smw = [("s_sc", p.ct["s_sc"])]
                mmd = p.seg("tensor", lambda te, i=i, kt=kt, q0=q0, ps=ps: [
                    te.matmul(ps[:, qh * 512:(qh + 1) * 512], kc[i][:, kt * 128:(kt + 1) * 128],
                              qc[i][:, q0 + qh * 512:q0 + (qh + 1) * 512], start=True, stop=True)
                    for qh in range(2)], waits=smw)
                exd = p.seg("scalar", lambda s, kt=kt, ps=ps: s.activation(
                    psb[:, :], ps[:, :], AF.Exp, bias=negbk[:, kt:kt + 1]),
                    waits=[("s_pe", p.ct["s_pe"]), ("s_sc", ebv)])
                pvd = p.seg("tensor", lambda te, i=i, kt=kt, ps=ps: [
                    te.matmul(p_pv[:, qh * 512:(qh + 1) * 512], vc[i][:, kt * 65:(kt + 1) * 65],
                              psb[:, qh * 512:(qh + 1) * 512], start=(kt == 0), stop=(kt == 31),
                              skip_group_check=True) for qh in range(2)],
                    waits=[("s_sc", exd), ("s_v", vsc)])
                pvw.append(pvd)
            rc = p.seg("vector", lambda v: v.reciprocal(rrow[0:1, :], p_pv[64:65, :]),
                       waits=[("s_pe", pvw[-1])])
            rp = p.seg("tensor", lambda te: [te.matmul(p_rep[:, qh * 512:(qh + 1) * 512],
                       p.t[p.pre + "ones_row"][0:1, 0:64], rrow[0:1, qh * 512:(qh + 1) * 512],
                       start=True, stop=True) for qh in range(2)], waits=[("s_v", rc)])
            rv = p.seg("vector", lambda v: v.tensor_copy(repb[:, :], p_rep[:, :]),
                       waits=[("s_pe", rp)])
            p.seg("vector", lambda v, q0=q0: v.tensor_tensor(ou[:, q0:q0 + 1024], p_pv[0:64, :],
                                                             repb[:, :], ALU.mult),
                  waits=[("s_v", rv)])
        stv = p.dma("sync", [(d_o[u], ou[:, :])], "s_st",
                    waits=[("s_v", p.ct["s_v"])])
        if debug and u == 0:
            dn2k = p.dout("dbg_n2k", [1, L], F32)
            dn2q = p.dout("dbg_n2q", [1, L], F32)
            dbkr = p.dout("dbg_bkr", [1, L], F32)
            dnbk = p.dout("dbg_negbk", [128, 32], F32)
            debc = p.dout("dbg_ebc", [128, 32], F32)
            dpsb = p.dout("dbg_psb", [128, 1024], BF16)
            drr = p.dout("dbg_rrow", [1, 1024], F32)
            drep = p.dout("dbg_repb", [64, 1024], BF16)
            p.dma("sync", [(dn2k, n2k[:, :]), (dn2q, n2q[:, :]), (dbkr, bkr[:, :]),
                           (dnbk, negbk[:, :]), (debc, ebc[:, :]), (dpsb, psb[:, :]),
                           (drr, rrow[:, :]), (drep, repb[:, :])], "s_st",
                  waits=[("s_v", p.ct["s_v"]), ("s_sc", p.ct["s_sc"])])
        if u + 2 < UPC:
            load_unit(u + 2)
    p.seg("sync", lambda s: s.wait_ge(p.sems["s_st"], p.ct["s_st"]), pub=False)
    if standalone:
        return p.finish()
    return None


def build_L7(p=None, io=None):
    standalone = p is None
    if standalone:
        p = Prog("L7")
    TB = 512
    NTB = TOK // TB
    d_x = dio(p, io, "x", [D, TOK], F32)
    d_oa = dio(p, io, "oa", [D, TOK], BF16)
    d_wo = dio(p, io, "wo", [D, D], F32)
    d_g2 = dio(p, io, "g2col", [128, 8], F32)
    d_b2 = dio(p, io, "b2col", [128, 8], F32)
    d_w1 = dio(p, io, "w1", [D, MLPD], F32)
    d_b1 = dio(p, io, "b1col", [128, 32], F32)
    d_b1r = dio(p, io, "b1row", [1, MLPD], F32)
    d_w2 = dio(p, io, "w2", [MLPD, D], F32)
    d_bb2 = dio(p, io, "bb2col", [128, 8], F32)
    d_x3 = dio(p, io, "x3", [D, TOK], F32, out=True)

    x = p.sb("b_x", [128, 8 * TB], F32)
    oa = p.sb("b_oa", [128, 8 * TB], BF16)
    wo = p.sb("b_wo", [128, 8 * D], BF16)
    xbb = p.sb("b_xbb", [128, 8 * TB], BF16)
    g2 = p.sb("b_g2", [128, 8], F32)
    b2 = p.sb("b_b2", [128, 8], F32)
    b1c = p.sb("b_b1c", [128, 32], F32)
    b1r = p.sb("b_b1r", [1, MLPD], F32)
    b1rb = p.sb("b_b1rb", [1, MLPD], BF16)
    onestb = p.sb("b_onestb", [1, TB], BF16)
    bb2 = p.sb("b_bb2", [128, 8], F32)
    h1 = p.sb("b_h1", [128, 32 * TB], BF16)
    w1 = [p.sb("b_w1%d" % i, [128, 32 * 128], BF16) for i in range(2)]
    w2 = w1
    p_mm = p.ps("p_mm", [128, 512])
    p_mm2 = p.ps("p_mm2", [128, 512])
    if standalone:
        p.sem("s_ld", "s_g", "s_g0", "s_g1", "s_pe", "s_v", "s_sc", "s_st")
        p.begin(consts=[1e-5, 1e-6])
    ob = ln_common(p)

    wl = p.dma("gpsimd", [(wo[:, :].rearrange("p (j o) -> p j o", j=8),
                           d_wo.rearrange("(j p) o -> p j o", p=128))], "s_g")
    cl = p.dma("sync", [(g2[:, :], d_g2), (b2[:, :], d_b2), (b1c[:, :], d_b1),
                        (bb2[:, :], d_bb2)], "s_ld")
    cl2 = p.dma("sync", [(b1r[0:1, k * 1024:(k + 1) * 1024], d_b1r[:, k * 1024:(k + 1) * 1024])
                         for k in range(4)], "s_ld", single_packet=True)
    b1cv = p.seg("vector", lambda v: (v.tensor_copy(b1rb[0:1, :], b1r[0:1, :]),
                                      v.memset(onestb[0:1, :], 1.0)),
                 waits=[("s_ld", cl2)])

    def load_w1(mb, waits=()):
        i = mb % 2
        return p.dma("gpsimd", [(w1[i][:, 0:1024].rearrange("p (j o) -> p j o", j=8),
                                 d_w1[:, mb * 128:(mb + 1) * 128].rearrange("(j p) o -> p j o", p=128))],
                     "s_g%d" % i, waits=list(waits))

    def load_w2(mb, waits=()):
        i = mb % 2
        return p.dma("gpsimd", [(w2[i][:, :].rearrange("p (j o) -> p j o", j=32),
                                 d_w2[:, mb * 128:(mb + 1) * 128].rearrange("(j p) o -> p j o", p=128))],
                     "s_g%d" % i, waits=list(waits))

    for tb in range(NTB):
        lx = p.dma("sync", [(x[:, :].rearrange("p (j w) -> p j w", j=8),
                             d_x[:, tb * TB:(tb + 1) * TB].rearrange("(j p) w -> p j w", p=128)),
                            (oa[:, :].rearrange("p (j w) -> p j w", j=8),
                             d_oa[:, tb * TB:(tb + 1) * TB].rearrange("(j p) w -> p j w", p=128))],
                   "s_ld", waits=[("s_v", p.ct["s_v"]), ("s_sc", p.ct["s_sc"])] if tb else ())
        ln3, _, ln3d = emit_ln(p, "a", x, TB, 1e-5, pre_wait=[("s_ld", lx)])

        hist = []
        for ot in range(8):
            pmm = p_mm if len(hist) % 2 == 0 else p_mm2
            wts = [("s_v", ln3d), ("s_g", wl)] if not hist else []
            if len(hist) >= 2:
                wts = wts + [("s_v", hist[-2])]
            mmd = p.seg("tensor", lambda te, ot=ot, pmm=pmm: [
                te.matmul(pmm[:, 0:TB], wo[:, (j * 8 + ot) * 128:(j * 8 + ot + 1) * 128],
                          ptile(oa, j, TB), start=(j == 0), stop=(j == 7))
                for j in range(8)], waits=wts)
            ev = p.seg("vector", lambda v, ot=ot, pmm=pmm: (
                v.tensor_tensor(ptile(xbb, ot, TB), pmm[:, 0:TB], ptile(ln3, ot, TB), ALU.add),
                v.drain(),
                v.tensor_tensor(ptile(x, ot, TB), ptile(x, ot, TB), ptile(ln3, ot, TB), ALU.add),
                v.drain(),
                v.tensor_tensor(ptile(x, ot, TB), ptile(x, ot, TB), pmm[:, 0:TB], ALU.add)),
                waits=[("s_pe", mmd)])
            hist.append(ev)
        wod = hist[-1]

        y0, _, ln2d = emit_ln(p, "b", xbb, TB, 1e-6, gcol=g2, bcol=b2,
                              pre_wait=[("s_v", wod), ("s_ld", p.ct["s_ld"])], src_bf=True)

        wl_pend = {0: load_w1(0, [("s_pe", p.ct["s_pe"])]), 1: load_w1(1, [("s_pe", p.ct["s_pe"])])}
        hist1 = []
        for mb in range(32):
            pmm = p_mm if len(hist1) % 2 == 0 else p_mm2
            wts = [("s_g%d" % (mb % 2), wl_pend[mb])]
            if not hist1:
                wts += [("s_v", ln2d), ("s_v", b1cv)]
            if len(hist1) >= 2:
                wts += [("s_sc", hist1[-2])]
            i = mb % 2
            mmd = p.seg("tensor", lambda te, i=i, mb=mb, pmm=pmm: (
                [te.matmul(pmm[:, 0:TB], w1[i][:, j * 128:(j + 1) * 128],
                           ptile(y0, j, TB), start=(j == 0), stop=False)
                 for j in range(8)],
                te.matmul(pmm[:, 0:TB], b1rb[0:1, mb * 128:(mb + 1) * 128],
                          onestb[0:1, 0:TB], start=False, stop=True)), waits=wts)
            ev = p.seg("scalar", lambda s, mb=mb, pmm=pmm: s.activation(
                h1[:, mb * TB:(mb + 1) * TB], pmm[:, 0:TB], AF.Relu),
                waits=[("s_pe", mmd)])
            hist1.append(ev)
            if mb + 2 < 32:
                wl_pend[mb + 2] = load_w1(mb + 2, [("s_pe", mmd)])
        l1d = hist1[-1]

        wl_pend = {0: load_w2(0, [("s_pe", p.ct["s_pe"])]), 1: load_w2(1, [("s_pe", p.ct["s_pe"])])}
        hist2 = []
        for mb in range(8):
            pmm = p_mm if len(hist2) % 2 == 0 else p_mm2
            wts = [("s_g%d" % (mb % 2), wl_pend[mb])]
            if not hist2:
                wts += [("s_sc", l1d)]
            if len(hist2) >= 2:
                wts += [("s_v", hist2[-2])]
            i = mb % 2
            mmd = p.seg("tensor", lambda te, i=i, pmm=pmm: [
                te.matmul(pmm[:, 0:TB], w2[i][:, j * 128:(j + 1) * 128],
                          h1[:, j * TB:(j + 1) * TB], start=(j == 0), stop=(j == 31))
                for j in range(32)], waits=wts)
            ev = p.seg("vector", lambda v, mb=mb, pmm=pmm: v.scalar_tensor_tensor(
                ptile(x, mb, TB), pmm[:, 0:TB], bb2[:, mb:mb + 1],
                ptile(x, mb, TB), ALU.add, ALU.add), waits=[("s_pe", mmd)])
            hist2.append(ev)
            if mb + 2 < 8:
                wl_pend[mb + 2] = load_w2(mb + 2, [("s_pe", mmd)])

        st = p.dma("sync", [(d_x3[:, tb * TB:(tb + 1) * TB].rearrange("(j p) w -> p j w", p=128),
                             x[:, :].rearrange("p (j w) -> p j w", j=8))], "s_st",
                   waits=[("s_v", p.ct["s_v"])])
        p.barrier("s_st", st)
    if standalone:
        return p.finish()
    return None


def l5_wh_img(wh):
    img = np.zeros((128, 8 * NB), np.float32)
    for h in range(H):
        img[(h % 2) * 64:(h % 2) * 64 + 64, (h // 2) * NB:(h // 2) * NB + NB] = np.asarray(wh[h], np.float32)
    return img


_PROGS = {}


def _prog(name, builder):
    if name not in _PROGS:
        _PROGS[name] = builder()
    return _PROGS[name]


def kernel(**inputs):
    inp = {k: np.asarray(v) for k, v in inputs.items()}
    x0 = inp["L0_attn_a_x"].astype(np.float32, copy=False)
    x_flat = np.ascontiguousarray(x0.reshape(B * L, D))
    cores = list(range(NCORE))

    wts = {n: inp["L0_attn_a_" + n] for n in
           ["Wk", "bk", "Wq", "bq", "Wv", "bv", "Wbeta", "bbeta", "conv_k", "conv_q", "conv_v"]}
    nc1 = _prog("L1", build_L1)
    r1 = run_bass_kernel_spmd(nc1, [l1_host_inputs(c, x_flat, wts) for c in cores], core_ids=cores)
    catf = lambda name: np.concatenate([np.asarray(r1.results[c][name]) for c in cores], 1)
    knF, qnF, bbF, vvF, btF = catf("kn"), catf("qn"), catf("bb"), catf("vv"), catf("beta")

    nc2 = _prog("L2", build_L2)
    r2 = run_bass_kernel_spmd(nc2, [l2_host_inputs(c, knF, qnF, bbF, vvF, btF,
                                                   inp["L0_attn_a_rms_w"]) for c in cores],
                              core_ids=cores)
    oF = np.zeros((D, B * L), dtype=np.float32)
    for c in cores:
        ou = l2_host_output(c, r2.results[c]["o"]).astype(np.float32)
        for i in range(UPC):
            b, h = (UPC * c + i) // H, (UPC * c + i) % H
            oF[h * DH:(h + 1) * DH, b * L:(b + 1) * L] = ou[i].T

    nc3 = _prog("L3", build_L3)
    in3 = [{"x": _f32(x_flat[c * TOK:(c + 1) * TOK].T),
            "of": _bf(oF[:, c * TOK:(c + 1) * TOK]),
            "woT": _f32(inp["L0_attn_a_Wout"].T),
            "bocol": _f32(inp["L0_attn_a_bout"].reshape(8, 128).T)} for c in cores]
    r3 = run_bass_kernel_spmd(nc3, in3, core_ids=cores)
    x1 = np.concatenate([np.asarray(r3.results[c]["x1"]) for c in cores], 1)

    k0 = np.ascontiguousarray(inp["L0_ffn_k0"].astype(np.uint8))
    k1u = np.ascontiguousarray(inp["L0_ffn_k1"].astype(np.uint8))
    k2u = np.ascontiguousarray(inp["L0_ffn_k2"].astype(np.uint8))
    sv = _f32(np.stack([inp["L0_ffn_s0"], inp["L0_ffn_s1"], inp["L0_ffn_s2"]]).reshape(1, 3))
    nc4 = _prog("L4", build_L4)

    def ffn_maps(xF):
        return [{"x": _f32(xF[:, c * TOK:(c + 1) * TOK]),
                 "k0t": k0[:D], "k0b": k0[D:], "k1": k1u, "k2": k2u, "svals": sv}
                for c in cores]

    r4 = run_bass_kernel_spmd(nc4, ffn_maps(x1), core_ids=cores)
    x2 = np.concatenate([np.asarray(r4.results[c]["xo"]) for c in cores], 1)

    nc5 = _prog("L5", build_L5)
    in5 = [{"x": _f32(x2[:, c * TOK:(c + 1) * TOK]),
            "wq": _f32(inp["L0_attn_b_Wq"].reshape(D, D)),
            "wk": _f32(inp["L0_attn_b_Wk"].reshape(D, D)),
            "wv": _f32(inp["L0_attn_b_Wv"].reshape(D, D)),
            "whq": l5_wh_img(inp["L0_attn_b_Whq"]),
            "whk": l5_wh_img(inp["L0_attn_b_Whk"]),
            "gcol": _f32(inp["L0_attn_b_ln1_g"].reshape(8, 128).T),
            "bcol": _f32(inp["L0_attn_b_ln1_b"].reshape(8, 128).T)} for c in cores]
    r5 = run_bass_kernel_spmd(nc5, in5, core_ids=cores)
    qfF = np.concatenate([np.asarray(r5.results[c]["qf"]) for c in cores], 1)
    kfF = np.concatenate([np.asarray(r5.results[c]["kf"]) for c in cores], 1)
    vfF = np.concatenate([np.asarray(r5.results[c]["vf"]) for c in cores], 1)

    def _qbu_decode(a):
        a = np.asarray(a).reshape(4, 128, 4, 512)
        return a.transpose(1, 2, 0, 3).reshape(128, 4, TOK)

    qbuC = [_qbu_decode(r5.results[c]["qbu"]) for c in cores]
    kbuC = [_qbu_decode(r5.results[c]["kbu"]) for c in cores]

    def qbu_head(lst, h, core):
        return lst[core][(h % 4) * 32:(h % 4) * 32 + 32, h // 4, :]

    nc6 = _prog("L6", build_L6)
    in6 = []
    for c in cores:
        qc = np.zeros((UPC, 96, L), dtype=BF)
        kc = np.zeros((UPC, 96, L), dtype=BF)
        vc = np.zeros((UPC, 128, 32 * 65), dtype=BF)
        for i in range(UPC):
            b, h = (UPC * c + i) // H, (UPC * c + i) % H
            qc[i, 0:64] = qfF[h * DH:(h + 1) * DH, b * L:(b + 1) * L]
            kc[i, 0:64] = kfF[h * DH:(h + 1) * DH, b * L:(b + 1) * L]
            qc[i, 64:96] = np.concatenate([qbu_head(qbuC, h, 2 * b), qbu_head(qbuC, h, 2 * b + 1)], 1)
            kc[i, 64:96] = np.concatenate([qbu_head(kbuC, h, 2 * b), qbu_head(kbuC, h, 2 * b + 1)], 1)
            v3 = vfF[h * DH:(h + 1) * DH, b * L:(b + 1) * L].T.reshape(32, 128, DH).transpose(1, 0, 2)
            vc[i] = np.concatenate([v3, np.zeros((128, 32, 1), v3.dtype)], 2).reshape(128, 32 * 65)
        in6.append({"qc": qc, "kc": kc, "vc": vc, "id32f": np.eye(32, dtype=np.float32)})
    r6 = run_bass_kernel_spmd(nc6, in6, core_ids=cores)
    oaF = np.zeros((D, B * L), np.float32)
    for c in cores:
        for i in range(UPC):
            b, h = (UPC * c + i) // H, (UPC * c + i) % H
            oaF[h * DH:(h + 1) * DH, b * L:(b + 1) * L] = np.asarray(r6.results[c]["o"][i],
                                                                     dtype=np.float32)

    nc7 = _prog("L7", build_L7)
    in7 = [{"x": _f32(x2[:, c * TOK:(c + 1) * TOK]),
            "oa": _bf(oaF[:, c * TOK:(c + 1) * TOK]),
            "wo": _f32(inp["L0_attn_b_Wo"].reshape(D, D)),
            "g2col": _f32(inp["L0_attn_b_ln2_g"].reshape(8, 128).T),
            "b2col": _f32(inp["L0_attn_b_ln2_b"].reshape(8, 128).T),
            "w1": _f32(inp["L0_attn_b_W1"]),
            "b1col": _f32(inp["L0_attn_b_b1"].reshape(32, 128).T),
            "w2": _f32(inp["L0_attn_b_W2"]),
            "bb2col": _f32(inp["L0_attn_b_b2"].reshape(8, 128).T)} for c in cores]
    r7 = run_bass_kernel_spmd(nc7, in7, core_ids=cores)
    x3 = np.concatenate([np.asarray(r7.results[c]["x3"]) for c in cores], 1)

    r8 = run_bass_kernel_spmd(nc4, ffn_maps(x3), core_ids=cores)
    x4 = np.concatenate([np.asarray(r8.results[c]["xo"]) for c in cores], 1)

    return np.ascontiguousarray(x4.T).reshape(B, L, D).astype(np.float32)


I32 = mybir.dt.int32
PAIRS = [[0, 1], [2, 3], [4, 5], [6, 7]]
XSEQ = L + HALO


def emit_P1(p, io):
    nblk = L // 512
    BW = 512
    BWH = BW + HALO
    d_x = dio(p, io, "xseq", [D, XSEQ], F32)
    d_wk = dio(p, io, "wk8", [D, 512], F32)
    d_wq = dio(p, io, "wq8", [D, 512], F32)
    d_wv = dio(p, io, "wv8", [D, 512], F32)
    d_wb = dio(p, io, "wb8", [D, UPC], F32)
    d_bias = dio(p, io, "bias8", [128, 4 * 3], F32)
    d_bbeta = dio(p, io, "bbeta8", [UPC, 1], F32)
    d_conv = dio(p, io, "conv8", [128, 4 * 4 * 3], F32)
    d_e8 = dio(p, io, "e8", [128, 4 * UPC], BF16)
    d_e8t = dio(p, io, "e8t", [UPC, 4 * 128], F32)
    d_id32 = dio(p, io, "id32f_a", [32, 32], F32)
    d_kf = io["img_kf"]
    d_qf = io["img_qf"]
    d_km = io["img_km"]
    d_bv = io["img_bv"]
    d_bt = io["img_bt"]

    x = p.sb("b_x", [128, 8 * BWH], F32)
    xbf = p.sb("b_xbf", [128, 8 * BWH], BF16)
    xln = p.sb("b_xln", [128, 8 * BWH], BF16)
    sq = p.sb("b_sq", [128, 8 * BWH], BF16)
    wk = p.sb("b_wk", [128, 8 * 512], BF16)
    wq = p.sb("b_wq", [128, 8 * 512], BF16)
    wv = p.sb("b_wv", [128, 8 * 512], BF16)
    wb = p.sb("b_wb", [128, 8 * UPC], BF16)
    bias = p.sb("b_bias", [128, 4 * 3], F32)
    bbeta = p.sb("b_bbeta", [UPC, 1], F32)
    convw = p.sb("b_convw", [128, 4 * 4 * 3], F32)
    e8 = p.sb("b_e8", [128, 4 * UPC], BF16)
    e8t = p.sb("b_e8t", [UPC, 4 * 128], F32)
    kp = p.sb("b_kp", [128, 4 * BWH], BF16)
    qp = p.sb("b_qp", [128, 4 * BWH], BF16)
    vp = p.sb("b_vp", [128, 4 * BWH], BF16)
    kc = p.sb("b_kc", [128, 4 * BW], BF16)
    qc = p.sb("b_qc", [128, 4 * BW], BF16)
    vc = p.sb("b_vc", [128, 4 * BW], BF16)
    ktm = p.sb("b_ktm", [128, 4 * BW], BF16)
    vtm = p.sb("b_vtm", [128, 4 * BW], BF16)
    bvs = p.sb("b_bvs", [128, 4 * 1024], BF16)
    betas = p.sb("b_betas", [UPC, BWH], F32)
    btm = p.sb("b_btm", [128, 4 * UPC], F32)
    srow = p.sb("b_srow", [1, BWH], F32)
    srow1 = p.sb("b_srow1", [1, BWH], F32)
    mu = p.sb("b_mu", [1, BWH], F32)
    rs = p.sb("b_rs", [1, BWH], F32)
    vtmp = p.sb("b_vtmp", [1, BWH], F32)
    ssk = p.sb("b_ssk", [UPC, BW], F32)
    rr = p.sb("b_rr", [UPC, BW], F32)
    onesb = p.sbr("ones_bf", [128, 1], BF16)
    onesr = p.sbr("ones_row", [1, 128], F32)

    p_stat = p.ps("p_stat", [1, 512])
    p_stat2 = p.ps("p_stat2", [1, 512])
    p_repA = p.ps("p_repA", [128, 512])
    p_repB = p.ps("p_repB", [128, 512])
    p_mm = p.ps("p_mm", [128, 512])
    p_mm2 = p.ps("p_mm2", [128, 512])
    p_ss = p.ps("p_ss", [UPC, 512])

    ob = p.seg("vector", lambda v: (v.memset(onesb[:, :], 1.0), v.memset(onesr[:, :], 1.0)))
    wl = p.dma("gpsimd", [
        (wk[:, :].rearrange("p (j o) -> p j o", j=8), d_wk.rearrange("(j p) o -> p j o", p=128)),
        (wq[:, :].rearrange("p (j o) -> p j o", j=8), d_wq.rearrange("(j p) o -> p j o", p=128)),
        (wv[:, :].rearrange("p (j o) -> p j o", j=8), d_wv.rearrange("(j p) o -> p j o", p=128)),
        (wb[:, :].rearrange("p (j h) -> p j h", j=8), d_wb.rearrange("(j p) h -> p j h", p=128))],
        "s_g")
    sl = p.dma("sync", [(bias[:, :], d_bias), (bbeta[:, :], d_bbeta), (convw[:, :], d_conv),
                        (e8[:, :], d_e8), (e8t[:, :], d_e8t)], "s_ld")
    i8 = p.seg("vector", lambda v: None, waits=[("s_ld", sl)])
    id32 = p.sb("b_id32a", [32, 32], F32)
    sl2 = p.dma("sync", [(id32[:, :], d_id32)], "s_ld")

    for blk in range(nblk):
        c0 = blk * BW
        lx = p.dma("sync", [(x[:, :].rearrange("p (j w) -> p j w", j=8),
                             d_x[:, c0:c0 + BWH].rearrange("(j p) w -> p j w", p=128))], "s_ld",
                   waits=[("s_v", p.ct["s_v"]), ("s_sc", p.ct["s_sc"])] if blk else ())
        cast = p.seg("vector", lambda v: [v.tensor_copy(ptile(xbf, j, BWH), ptile(x, j, BWH))
                                          for j in range(8)], waits=[("s_ld", lx)])
        sqd = p.seg("scalar", lambda s: [s.activation(ptile(sq, j, BWH), ptile(xbf, j, BWH),
                                                      AF.Square) for j in range(8)],
                    waits=[("s_v", cast)])
        sums_done = None
        for (cc, cw) in col_chunks(BWH):
            mmv = p.seg("tensor", lambda te, cc=cc, cw=cw: (
                [te.matmul(p_stat[0:1, 0:cw], onesb[:, 0:1], ptile(xbf, j, BWH, cc, cw),
                           start=(j == 0), stop=(j == 7)) for j in range(8)],
                [te.matmul(p_stat2[0:1, 0:cw], onesb[:, 0:1], ptile(sq, j, BWH, cc, cw),
                           start=(j == 0), stop=(j == 7)) for j in range(8)]),
                waits=[("s_sc", sqd), ("s_v", max(ob, p.ct["s_v"]))])
            sums_done = p.seg("vector", lambda v, cc=cc, cw=cw: (
                v.tensor_copy(srow[0:1, cc:cc + cw], p_stat[0:1, 0:cw]),
                v.tensor_copy(srow1[0:1, cc:cc + cw], p_stat2[0:1, 0:cw])),
                waits=[("s_pe", mmv)])
        vard = p.seg("vector", lambda v: (
            v.tensor_scalar(mu[0:1, :], srow[0:1, :], 1.0 / D, None, ALU.mult),
            v.tensor_scalar(vtmp[0:1, :], srow1[0:1, :], 1.0 / D, None, ALU.mult),
            v.drain(),
            v.tensor_tensor(srow[0:1, :], mu[0:1, :], mu[0:1, :], ALU.mult),
            v.drain(),
            v.tensor_tensor(vtmp[0:1, :], vtmp[0:1, :], srow[0:1, :], ALU.subtract)),
            waits=[("s_v", sums_done)])
        rsd = p.seg("scalar", lambda s: (
            s.activation(rs[0:1, :], vtmp[0:1, :], AF.Ln, bias=p.cst(1e-5, 1)),
            s.drain(),
            s.activation(rs[0:1, :], rs[0:1, :], AF.Exp, scale=-0.5)),
            waits=[("s_v", vard)])
        xln_done = None
        for (cc, cw) in col_chunks(BWH):
            repd = p.seg("tensor", lambda te, cc=cc, cw=cw: (
                te.matmul(p_repA[:, 0:cw], onesr[0:1, :], mu[0:1, cc:cc + cw], start=True, stop=True),
                te.matmul(p_repB[:, 0:cw], onesr[0:1, :], rs[0:1, cc:cc + cw], start=True, stop=True)),
                waits=[("s_sc", rsd), ("s_v", p.ct["s_v"])])
            xln_done = p.seg("vector", lambda v, cc=cc, cw=cw: (
                [v.tensor_tensor(ptile(xln, j, BWH, cc, cw), ptile(xbf, j, BWH, cc, cw),
                                 p_repA[:, 0:cw], ALU.subtract) for j in range(8)],
                v.drain(),
                [v.tensor_tensor(ptile(xln, j, BWH, cc, cw), ptile(xln, j, BWH, cc, cw),
                                 p_repB[:, 0:cw], ALU.mult) for j in range(8)]),
                waits=[("s_pe", repd)])

        evh = []
        for pi, (wsb, dst) in enumerate([(wk, kp), (wq, qp), (wv, vp)]):
            for ot in range(4):
                for ci, (cc, cw) in enumerate(col_chunks(BWH)):
                    pmm = p_mm if len(evh) % 2 == 0 else p_mm2
                    wts = [("s_v", xln_done), ("s_g", wl)] if not evh else []
                    if len(evh) >= 2:
                        wts = wts + [("s_sc", evh[-2])]
                    mmd = p.seg("tensor", lambda te, wsb=wsb, ot=ot, cc=cc, cw=cw, pmm=pmm: [
                        te.matmul(pmm[:, 0:cw], wsb[:, (j * 4 + ot) * 128:(j * 4 + ot + 1) * 128],
                                  ptile(xln, j, BWH, cc, cw), start=(j == 0), stop=(j == 7))
                        for j in range(8)], waits=wts)
                    ev = p.seg("scalar", lambda s, dst=dst, ot=ot, cc=cc, cw=cw, pi=pi, pmm=pmm:
                               s.activation(ptile(dst, ot, BWH, cc, cw), pmm[:, 0:cw], AF.Identity,
                                            bias=bias[:, pi * 4 + ot:pi * 4 + ot + 1]),
                               waits=[("s_pe", mmd)])
                    evh.append(ev)
        projd = evh[-1]

        betad = None
        for (cc, cw) in col_chunks(BWH):
            bmm = p.seg("tensor", lambda te, cc=cc, cw=cw: [
                te.matmul(p_ss[:, 0:cw], wb[:, j * UPC:(j + 1) * UPC], ptile(xln, j, BWH, cc, cw),
                          start=(j == 0), stop=(j == 7)) for j in range(8)],
                waits=[("s_sc", p.ct["s_sc"]), ("s_v", p.ct["s_v"])])
            betad = p.seg("scalar", lambda s, cc=cc, cw=cw: s.activation(
                betas[:, cc:cc + cw], p_ss[:, 0:cw], AF.Sigmoid, bias=bbeta[:, 0:1]),
                waits=[("s_pe", bmm)])

        halo_v = p.ct["s_v"]
        if blk == 0:
            halo_v = p.seg("vector", lambda v: [
                v.memset(ptile(d2, jj, BWH, HALO - 3, 3), 0.0)
                for d2 in (kp, qp, vp) for jj in range(4)],
                waits=[("s_sc", projd)])

        def conv_body(v):
            for pi2, (d1, d2) in enumerate([(kp, kc), (qp, qc), (vp, vc)]):
                for jj in range(4):
                    v.tensor_scalar(ptile(d2, jj, BW), ptile(d1, jj, BWH, HALO - 3, BW),
                                    convw[:, jj * 12 + pi2 * 4:jj * 12 + pi2 * 4 + 1], None, ALU.mult)
                    v.drain()
                    for tap in range(1, 4):
                        v.scalar_tensor_tensor(ptile(d2, jj, BW), ptile(d1, jj, BWH, HALO - 3 + tap, BW),
                                               convw[:, jj * 12 + pi2 * 4 + tap:jj * 12 + pi2 * 4 + tap + 1],
                                               ptile(d2, jj, BW), ALU.mult, ALU.add)
                        v.drain()
        convd = p.seg("vector", conv_body, waits=[("s_sc", projd), ("s_v", halo_v)])

        for tag2, src2 in (("k", kc), ("q", qc)):
            sq2 = p.seg("scalar", lambda s, src2=src2: [
                s.activation(ptile(sq, jj, BWH, 0, BW), ptile(src2, jj, BW), AF.Square)
                for jj in range(4)], waits=[("s_v", convd)])
            ssd = p.seg("tensor", lambda te: [
                te.matmul(p_ss[:, 0:BW], e8[:, jj * UPC:(jj + 1) * UPC], ptile(sq, jj, BWH, 0, BW),
                          start=(jj == 0), stop=(jj == 3)) for jj in range(4)],
                waits=[("s_sc", sq2)])
            ssv = p.seg("vector", lambda v: v.tensor_copy(ssk[:, :], p_ss[:, 0:BW]),
                        waits=[("s_pe", ssd)])
            sqr = p.seg("scalar", lambda s: (
                s.activation(rr[:, :], ssk[:, :], AF.Ln, bias=p.cst(1e-30, UPC)),
                s.drain(),
                s.activation(rr[:, :], rr[:, :], AF.Exp, scale=0.5)),
                waits=[("s_v", ssv)])
            rin = p.seg("vector", lambda v: (
                v.tensor_scalar(rr[:, :], rr[:, :], 1e-6, None, ALU.add),
                v.drain(),
                v.reciprocal(rr[:, :], rr[:, :])),
                waits=[("s_sc", sqr)])
            last = rin
            for jj in range(4):
                pr = p_repA if jj % 2 == 0 else p_repB
                rd = p.seg("tensor", lambda te, jj=jj, pr=pr: te.matmul(
                    pr[:, 0:BW], e8t[:, jj * 128:(jj + 1) * 128], rr[:, 0:BW], start=True, stop=True),
                    waits=[("s_v", last if jj < 2 else p.ct["s_v"])])
                last = p.seg("vector", lambda v, jj=jj, pr=pr, src2=src2: v.tensor_tensor(
                    ptile(src2, jj, BW), ptile(src2, jj, BW), pr[:, 0:BW], ALU.mult),
                    waits=[("s_pe", rd)])
        normd = p.ct["s_v"]
        sil = p.seg("scalar", lambda s: [
            s.activation(ptile(vc, jj, BW), ptile(vc, jj, BW), AF.Silu) for jj in range(4)],
            waits=[("s_v", convd)])

        trp = []
        for srct, dstt in ((kc, ktm), (vc, vtm)):
            for jt in range(4):
                for tt in range(4):
                    trp.append((dstt[:, tt * BW + jt * 128:tt * BW + (jt + 1) * 128],
                                srct[:, jt * BW + tt * 128:jt * BW + (tt + 1) * 128]))
        trd = p.dma("sync", [(o_, i_) for (o_, i_) in trp], "s_bk",
                    waits=[("s_v", normd), ("s_sc", sil), ("s_bk", p.ct["s_bk"])],
                    transpose=True)
        btr = []
        for tt in range(4):
            r1 = p.seg("tensor", lambda te, tt=tt: te.transpose(
                p_repA[:, 0:UPC], betas[:, HALO + tt * 128:HALO + (tt + 1) * 128],
                id32[0:UPC, 0:UPC]),
                waits=[("s_sc", p.ct["s_sc"]), ("s_v", p.ct["s_v"]), ("s_ld", sl2)])
            btr.append(p.seg("vector", lambda v, tt=tt: v.tensor_copy(
                btm[:, tt * UPC:(tt + 1) * UPC], p_repA[:, 0:UPC]), waits=[("s_pe", r1)]))
        if blk == 0:
            pass
        bvd = p.seg("vector", lambda v: [
            (v.tensor_scalar(bvs[:, tt * 1024 + u * 128:tt * 1024 + u * 128 + 64],
                             ktm[:, tt * BW + u * 64:tt * BW + (u + 1) * 64],
                             btm[:, tt * UPC + u:tt * UPC + u + 1], None, ALU.mult),
             v.tensor_scalar(bvs[:, tt * 1024 + u * 128 + 64:tt * 1024 + (u + 1) * 128],
                             vtm[:, tt * BW + u * 64:tt * BW + (u + 1) * 64],
                             btm[:, tt * UPC + u:tt * UPC + u + 1], None, ALU.mult))
            for tt in range(4) for u in range(UPC)],
            waits=[("s_bk", trd), ("s_v", btr[-1])])

        wrts = []
        for u in range(UPC):
            po = (u % 2) * 64
            jt = u // 2
            wrts.append((d_kf[blk][:, u * 512:(u + 1) * 512], kc[po:po + 64, jt * BW:(jt + 1) * BW]))
            wrts.append((d_qf[blk][:, u * 512:(u + 1) * 512], qc[po:po + 64, jt * BW:(jt + 1) * BW]))
        for tt in range(4):
            for ph in range(2):
                ccix = 2 * tt + ph
                wrts.append((d_km[blk].rearrange("r (u cw) -> r u cw", u=8)[:, :, ccix * 64:(ccix + 1) * 64],
                             ktm[ph * 64:ph * 64 + 64, tt * BW:(tt + 1) * BW]
                             .rearrange("r (u cw) -> r u cw", u=8)))
                wrts.append((d_bv[blk].rearrange("r (u cw) -> r u cw", u=8)[:, :, ccix * 128:(ccix + 1) * 128],
                             bvs[ph * 64:ph * 64 + 64, tt * 1024:(tt + 1) * 1024]
                             .rearrange("r (u cw) -> r u cw", u=8)))
                wrts.append((d_bt.rearrange("r (c u) -> r c u", u=8)[:, 8 * blk + ccix, :],
                             btm[ph * 64:ph * 64 + 64, tt * UPC:(tt + 1) * UPC]))
        stv = p.dma("sync", wrts, "s_st", waits=[("s_v", bvd)])
        p.barrier("s_st", stv)

    return None

def emit_P3(p, io):
    d_ago = io["ago"]
    cc_ag = io["cc_oag"]
    d_oidx = dio(p, io, "oidx", [128, 32], I32)
    d_x0 = dio(p, io, "x0own", [D, TOK], F32)
    d_wo = dio(p, io, "woT", [D, D], F32)
    d_bo = dio(p, io, "bocol", [128, 8], F32)
    d_x1 = io["x1"]
    TB = 512
    ago_rows = d_ago.rearrange("s c r w -> (s c r) w")

    oidx = p.sb("b_oidx", [128, 32], I32)
    wo = p.sb("b_wo", [128, 8 * D], BF16)
    bo = p.sb("b_bo", [128, 8], F32)
    x = p.sb("b_x", [128, 8 * TB], F32)
    ofm = p.sb("b_ofm", [128, 8 * TB], BF16)
    ostg = [p.sb("b_ostg%d" % i, [128, 1024], BF16) for i in range(2)]
    p_mm = p.ps("p_mm", [128, 512])
    p_mm2 = p.ps("p_mm2", [128, 512])

    wl = p.dma("gpsimd", [(wo[:, :].rearrange("p (j o) -> p j o", j=8),
                           d_wo.rearrange("(j p) o -> p j o", p=128))], "s_g")
    cl = p.dma("sync", [(oidx[:, :], d_oidx), (bo[:, :], d_bo)], "s_ld")

    prev_st = 0
    prev_ev = 0
    prev_tr = 0
    mm_last = 0
    for tb in range(4):
        lx = p.dma("sync", [(x[:, :].rearrange("p (j w) -> p j w", j=8),
                             d_x0[:, tb * TB:(tb + 1) * TB].rearrange("(j p) w -> p j w", p=128))],
                   "s_ld", waits=[("s_v", prev_ev), ("s_st", prev_st)])
        trs = {}
        for k2 in range(4):
            j = tb * 4 + k2
            sb_ = ostg[k2 % 2]
            gw = [("s_cc", cc_ag), ("s_ld", cl)] if (tb == 0 and k2 == 0) else []
            if k2 >= 2:
                gw = gw + [("s_bk", trs[k2 - 2])]
            elif tb > 0:
                gw = gw + [("s_bk", prev_tr)]
            gd = p.idma([(sb_[:, s * 512:(s + 1) * 512], ago_rows,
                          oidx[:, s * 16 + j:s * 16 + j + 1]) for s in range(2)], "s_g",
                        waits=gw)
            tw = [("s_g", gd)]
            if k2 == 0 and tb > 0:
                tw.append(("s_pe", mm_last))
            trs[k2] = p.dma("scalar", [
                (ofm[:, (s * 4 + pi) * TB + k2 * 128:(s * 4 + pi) * TB + (k2 + 1) * 128],
                 sb_[:, s * 512 + pi * 128:s * 512 + (pi + 1) * 128])
                for s in range(2) for pi in range(4)], "s_bk", transpose=True, waits=tw)
        prev_tr = trs[3]
        hist = []
        for ot in range(8):
            pmm = p_mm if len(hist) % 2 == 0 else p_mm2
            wts = [("s_bk", trs[3]), ("s_ld", lx), ("s_g", wl)] if not hist else []
            if len(hist) >= 2:
                wts = wts + [("s_v", hist[-2])]
            mmd = p.seg("tensor", lambda te, ot=ot, pmm=pmm: [
                te.matmul(pmm[:, 0:TB], wo[:, (j2 * 8 + ot) * 128:(j2 * 8 + ot + 1) * 128],
                          ptile(ofm, j2, TB), start=(j2 == 0), stop=(j2 == 7))
                for j2 in range(8)], waits=wts)
            ev = p.seg("vector", lambda v, ot=ot, pmm=pmm: v.scalar_tensor_tensor(
                ptile(x, ot, TB), pmm[:, 0:TB], bo[:, ot:ot + 1], ptile(x, ot, TB),
                ALU.add, ALU.add), waits=[("s_pe", mmd), ("s_ld", lx)])
            hist.append(ev)
        mm_last = p.ct["s_pe"]
        prev_ev = hist[-1]
        prev_st = p.dma("sync", [(d_x1[:, tb * TB:(tb + 1) * TB].rearrange("(j p) w -> p j w", p=128),
                                  x[:, :].rearrange("p (j w) -> p j w", j=8))], "s_st",
                        waits=[("s_v", prev_ev)])
    p.seg("sync", lambda s: s.wait_ge(p.sems["s_st"], p.ct["s_st"]), pub=False)


def emit_P6(p, io):
    d_qf = io["qf"]
    d_qbu = io["qbu"]
    d_kfa = io["kfa"]
    d_kbua = io["kbua"]
    d_vfa = io["vfa"]
    cc_kv = io["cc_kv"]
    d_id32 = dio(p, io, "id32f_b", [32, 32], F32)
    d_oat = io["oattn"]
    d_bkt = p.dtmp("bk_bounce_f", [1, L], BF16)

    qc = [p.sb("b_qc%d" % i, [97, TOK], BF16) for i in range(2)]
    kc = [p.sb("b_kc%d" % i, [97, L], BF16) for i in range(2)]
    vc = [p.sb("b_vc%d" % i, [128, 32 * 128], BF16) for i in range(2)]
    sqk = p.sb("b_sqk", [96, L], BF16)
    n2k = p.sb("b_n2k", [1, L], F32)
    n2q = p.sb("b_n2q", [1, L], F32)
    mq2 = p.sb("b_mq2", [1, 2], F32)
    bkr = p.sb("b_bkr", [1, L], F32)
    negbk = p.sb("b_negbk", [128, 32], BF16)
    ebc = p.sb("b_ebc", [128, 32], F32)
    id32 = p.sb("b_id32", [32, 32], F32)

    psb = [p.sb("b_psb%d" % i, [128, 1024], BF16) for i in range(2)]
    repb = p.sb("b_repb", [64, 1024], BF16)
    ou = p.sb("b_ou", [C, TOK], BF16)
    rrow = p.sb("b_rrow", [1, 1024], F32)
    p_s = [p.ps("p_s%d" % i, [128, 1024]) for i in range(2)]
    p_pv = p.ps("p_pv", [65, 1024])
    p_rep = p.ps("p_rep", [64, 1024])
    p_row = p_rep
    ob = ln_common(p)
    onesb = p.t[p.pre + "ones_bf"]
    onesr = p.t[p.pre + "ones_row"]
    ld0 = p.dma("sync", [(id32[:, :], d_id32)], "s_ld")
    q1i = p.seg("vector", lambda v: [v.memset(qc[i][96:97, :], 1.0) for i in range(2)])

    hload = {}

    def load_head(h):
        i = h % 2
        pl = [(qc[i][0:64, :], d_qf[h * DH:(h + 1) * DH, :])]
        pl += [(qc[i][64:96, t4 * 512:(t4 + 1) * 512],
                d_qbu[t4][(h % 4) * 32:(h % 4) * 32 + 32, (h // 4) * 512:(h // 4 + 1) * 512])
               for t4 in range(4)]
        pl += [(kc[i][0:64, rk * TOK:(rk + 1) * TOK], d_kfa[rk][h * DH:(h + 1) * DH, :])
               for rk in range(2)]
        pl += [(kc[i][64:96, rk * TOK + t4 * 512:rk * TOK + (t4 + 1) * 512],
                d_kbua[rk][t4][(h % 4) * 32:(h % 4) * 32 + 32, (h // 4) * 512:(h // 4 + 1) * 512])
               for rk in range(2) for t4 in range(4)]
        w0 = ([("s_pe", p.ct["s_pe"]), ("s_sc", p.ct["s_sc"]), ("s_v", p.ct["s_v"])]
              if h >= 2 else [("s_cc", cc_kv)])
        p.dma("sync", pl, "s_w%d" % i, waits=w0)
        trp = [(vc[i][:, (rk * 16 + tt) * 128:(rk * 16 + tt) * 128 + 64],
                d_vfa[rk][h * DH:(h + 1) * DH, tt * 128:(tt + 1) * 128])
               for rk in range(2) for tt in range(16)]
        hload[h] = p.dma("sync", trp, "s_w%d" % i, transpose=True)

    load_head(0)
    load_head(1)
    if io.get("dbgp6"):
        dq = p.dout("dbg_qc0", [96, TOK], BF16)
        dk = p.dout("dbg_kc0", [96, L], BF16)
        dv = p.dout("dbg_vc0", [128, 32 * 128], BF16)
        vz = p.seg("vector", lambda v: [v.memset(vc[0][:, kt * 128 + 64:kt * 128 + 128], 0.0)
                                        for kt in range(32)], waits=[("s_w0", hload[0])])
        p.dma("sync", [(dq, qc[0][0:96, :]), (dk, kc[0][0:96, :]), (dv, vc[0][:, :])], "s_st",
              waits=[("s_w0", hload[0]), ("s_v", vz)])

    for h in range(H):
        i = h % 2
        uw = [("s_w%d" % i, hload[h])]
        sq1 = p.seg("scalar", lambda s, i=i: [s.activation(sqk[:, cc:cc + 512], kc[i][0:96, cc:cc + 512],
                                                           AF.Square) for cc in range(0, L, 512)],
                    waits=uw + [("s_v", p.ct["s_v"])])
        nk = None
        for cc in range(0, L, 512):
            m1 = p.seg("tensor", lambda te, cc=cc: te.matmul(
                p_row[0:1, 0:512], onesb[0:96, 0:1], sqk[:, cc:cc + 512],
                start=True, stop=True), waits=[("s_sc", sq1), ("s_v", p.ct["s_v"])])
            nk = p.seg("vector", lambda v, cc=cc: v.tensor_copy(n2k[0:1, cc:cc + 512],
                                                                p_row[0:1, 0:512]),
                       waits=[("s_pe", m1)])
        sq2 = p.seg("scalar", lambda s, i=i: [s.activation(sqk[:, cc:cc + 512], qc[i][0:96, cc:cc + 512],
                                                           AF.Square) for cc in range(0, TOK, 512)],
                    waits=uw + [("s_v", nk)])
        nq = None
        for cc in range(0, TOK, 512):
            m1 = p.seg("tensor", lambda te, cc=cc: te.matmul(
                p_row[0:1, 0:512], onesb[0:96, 0:1], sqk[:, cc:cc + 512],
                start=True, stop=True), waits=[("s_sc", sq2), ("s_v", p.ct["s_v"])])
            nq = p.seg("vector", lambda v, cc=cc: v.tensor_copy(n2q[0:1, cc:cc + 512],
                                                                p_row[0:1, 0:512]),
                       waits=[("s_pe", m1)])
        mx = p.seg("vector", lambda v: v.reduce_max(mq2[0:1, 0:1], n2q[0:1, 0:TOK], AX.X, None),
                   waits=[("s_v", nq)])
        lnm = p.seg("scalar", lambda s: (
            s.activation(mq2[0:1, 1:2], mq2[0:1, 0:1], AF.Ln, bias=p.cst(1e-30, 1)),
            s.drain(),
            s.activation(bkr[0:1, :], n2k[0:1, :], AF.Ln, bias=p.cst(1e-30, 1))),
            waits=[("s_v", mx)])
        hb = p.seg("vector", lambda v: v.tensor_scalar(mq2[0:1, 1:2], mq2[0:1, 1:2], 0.5,
                                                       None, ALU.mult), waits=[("s_sc", lnm)])
        bk1 = p.seg("scalar", lambda s, i=i: (
            s.activation(bkr[0:1, :], bkr[0:1, :], AF.Exp, scale=0.5, bias=mq2[0:1, 1:2]),
            s.drain(),
            s.activation(kc[i][96:97, :], bkr[0:1, :], AF.Copy, scale=-1.0)),
            waits=[("s_v", hb), ("s_v", q1i)])
        bs1 = p.dma("sync", [(d_bkt, kc[i][96:97, :])], "s_bk",
                    waits=[("s_sc", bk1)], single_packet=True)
        bs2 = p.dma("sync", [(negbk[:, :], d_bkt.rearrange("a (k q) -> (a k) q", k=32))],
                    "s_bk", transpose=True)
        ebv = p.seg("scalar", lambda s, i=i: (
            s.activation(ebc[:, :], negbk[:, :], AF.Exp, scale=-1.0),
            s.drain(),
            [s.copy(vc[i][:, kt * 128 + 64:kt * 128 + 65], ebc[:, kt:kt + 1]) for kt in range(32)]),
            waits=[("s_bk", bs2), ("s_w%d" % i, hload[h])])
        vsc = p.seg("vector", lambda v, i=i: [v.tensor_scalar(
            vc[i][:, kt * 128:kt * 128 + 64], vc[i][:, kt * 128:kt * 128 + 64],
            ebc[:, kt:kt + 1], None, ALU.mult) for kt in range(32)],
            waits=[("s_sc", ebv)])

        for qb in range(2):
            q0 = qb * 1024
            pe_base = p.ct["s_pe"]
            sc_base = p.ct["s_sc"]
            exw = {kt: sc_base + kt + 1 for kt in range(32)}

            def mega(s, pe_base=pe_base):
                for kt in range(32):
                    s.wait_ge(p.sems["s_pe"], pe_base + (kt + 1 if kt < 2 else 2 * kt))
                    s.activation(psb[kt % 2][:, :], p_s[kt % 2][:, :],
                                 AF.Exp).then_inc(p.sems["s_sc"], 1)
            p.block.scalar(mega)
            p.ct["s_sc"] += 32

            mmw, pvw = {}, {}

            def emit_mm(kt, extra, i=i, q0=q0):
                ps = p_s[kt % 2]
                mmw[kt] = p.seg("tensor", lambda te, kt=kt, ps=ps: [
                    te.matmul(ps[:, qh * 512:(qh + 1) * 512], kc[i][:, kt * 128:(kt + 1) * 128],
                              qc[i][:, q0 + qh * 512:q0 + (qh + 1) * 512], start=True, stop=True)
                    for qh in range(2)], waits=extra)
            emit_mm(0, [("s_sc", sc_base), ("s_v", p.ct["s_v"])])
            emit_mm(1, [])
            for kt in range(32):
                pb = psb[kt % 2]
                if kt + 2 < 32:
                    emit_mm(kt + 2, [("s_sc", exw[kt])])
                pvw[kt] = p.seg("tensor", lambda te, i=i, kt=kt, pb=pb: [
                    te.matmul(p_pv[:, qh * 512:(qh + 1) * 512], vc[i][:, kt * 128:kt * 128 + 65],
                              pb[:, qh * 512:(qh + 1) * 512], start=(kt == 0), stop=(kt == 31),
                              skip_group_check=True) for qh in range(2)],
                    waits=[("s_sc", exw[kt]), ("s_v", vsc)] if kt == 0 else [("s_sc", exw[kt])])
            assert p.ct["s_pe"] == pe_base + 64, p.ct["s_pe"] - pe_base
            rc = p.seg("vector", lambda v: v.reciprocal(rrow[0:1, :], p_pv[64:65, :]),
                       waits=[("s_pe", pvw[31])])
            rp = p.seg("tensor", lambda te: [te.matmul(p_rep[:, qh * 512:(qh + 1) * 512],
                       onesr[0:1, 0:64], rrow[0:1, qh * 512:(qh + 1) * 512],
                       start=True, stop=True) for qh in range(2)], waits=[("s_v", rc)])
            rv = p.seg("vector", lambda v: v.tensor_copy(repb[:, :], p_rep[:, :]),
                       waits=[("s_pe", rp)])
            p.seg("vector", lambda v, q0=q0: v.tensor_tensor(ou[:, q0:q0 + 1024], p_pv[0:64, :],
                                                             repb[:, :], ALU.mult),
                  waits=[("s_v", rv)])
        p.dma("sync", [(d_oat[h * DH:(h + 1) * DH, :], ou[:, :])], "s_st",
              waits=[("s_v", p.ct["s_v"])])
        if h + 2 < H:
            load_head(h + 2)
    p.seg("sync", lambda s: s.wait_ge(p.sems["s_st"], p.ct["s_st"]), pub=False)


def build_fused(upto=8, dbg=False):
    p = Prog("FUSED")
    p.sem(*FUSED_SEMS)

    def dram(name, shape, dt, internal=False):
        if dbg and not internal:
            return p.nc.dram_tensor(name, shape, dt, kind="ExternalOutput")
        return p.nc.dram_tensor(name, shape, dt)

    t_oloc = dram("oloc", [NCHUNK, C, 512], BF16, internal=True)
    t_ago = dram("ago", [2, NCHUNK, C, 512], BF16, internal=True)
    t_kfin = dram("kfin", [D, TOK], BF16, internal=True)
    t_kbuin = dram("kbuin", [4, 128, 4 * 512], BF16, internal=True)
    t_vfin = dram("vfin", [D, TOK], BF16, internal=True)
    t_kfa = dram("kfa", [2, D, TOK], BF16, internal=True)
    t_kbua = dram("kbua", [2, 4, 128, 4 * 512], BF16, internal=True)
    t_vfa = dram("vfa", [2, D, TOK], BF16, internal=True)
    t_imgkf = dram("img_kf", [NG, C, UPC * 512], BF16)
    t_imgqf = dram("img_qf", [NG, C, UPC * 512], BF16)
    t_imgkm = dram("img_km", [NG, C, UPC * 512], BF16)
    t_imgbv = dram("img_bv", [NG, C, UPC * 1024], BF16)
    t_imgbt = dram("img_bt", [C, NCHUNK * UPC], F32)
    t_x1 = dram("x1s", [D, TOK], F32)
    t_x2 = dram("x2s", [D, TOK], F32)
    t_x3 = dram("x3s", [D, TOK], F32)
    t_qfs = dram("qfs", [D, TOK], BF16)
    t_qbus = dram("qbus", [4, 128, 4 * 512], BF16)
    t_oat = dram("oats", [D, TOK], BF16)
    d_xo = p.dout("xo", [D, TOK], F32)
    d_k0t = p.din("k0t", [D, MLPD], U8)
    d_k0b = p.din("k0b", [D, MLPD], U8)
    d_k1 = p.din("k1", [MLPD, MLPD], U8)
    d_k2 = p.din("k2", [MLPD, D], U8)
    d_sv = p.din("svals", [1, 3], F32)
    ffn_io = {"k0t": d_k0t, "k0b": d_k0b, "k1": d_k1, "k2": d_k2, "svals": d_sv}

    p.begin(consts=FUSED_CONSTS)

    p.pre = "a_"
    emit_P1(p, {"img_kf": t_imgkf.ap(), "img_qf": t_imgqf.ap(), "img_km": t_imgkm.ap(),
                "img_bv": t_imgbv.ap(), "img_bt": t_imgbt.ap()})
    p.phase_barrier()
    p.end_phase()
    if upto >= 2:
        p.pre = "b_"
        build_L2(p=p, io={"kf": t_imgkf.ap(), "qf": t_imgqf.ap(), "km": t_imgkm.ap(),
                          "bv": t_imgbv.ap(), "betat": t_imgbt.ap(), "o": t_oloc.ap()})
        p.phase_barrier()
        p.end_phase()
    if upto >= 3:
        cc_oag = p.collective([t_oloc.ap().opt()], [t_ago.ap().opt()],
                              waits=[("s_st", p.ct["s_st"])])
        p.pre = "c_"
        emit_P3(p, {"ago": t_ago.ap(), "cc_oag": cc_oag, "x1": t_x1.ap()})
        p.phase_barrier()
        p.end_phase()
    if upto >= 4:
        p.pre = "d_"
        io4 = dict(ffn_io)
        io4.update({"x": t_x1.ap(), "xo": t_x2.ap()})
        build_L4(p=p, io=io4, tag="p4")
        p.phase_barrier()
        p.end_phase()
    if upto >= 5:
        p.pre = "e_"
        build_L5(p=p, io={"x": t_x2.ap(), "qf": t_qfs.ap(), "kf": t_kfin.ap(),
                          "vf": t_vfin.ap(), "qbu": t_qbus.ap(), "kbu": t_kbuin.ap()})
        p.phase_barrier()
        p.end_phase()
    if upto >= 6:
        p.collective([t_kfin.ap().opt()], [t_kfa.ap().opt()],
                     waits=[("s_st", p.ct["s_st"])])
        p.collective([t_kbuin.ap().opt()], [t_kbua.ap().opt()])
        cc_kv = p.collective([t_vfin.ap().opt()], [t_vfa.ap().opt()])
        p.pre = "f_"
        emit_P6(p, {"qf": t_qfs.ap(), "qbu": t_qbus.ap(), "kfa": t_kfa.ap(),
                    "kbua": t_kbua.ap(), "vfa": t_vfa.ap(), "cc_kv": cc_kv,
                    "oattn": t_oat.ap(), "dbgp6": dbg})
        p.phase_barrier()
        p.end_phase()
    if upto >= 7:
        p.pre = "g_"
        build_L7(p=p, io={"x": t_x2.ap(), "oa": t_oat.ap(), "x3": t_x3.ap()})
        p.phase_barrier()
        p.end_phase()
    if upto >= 8:
        p.pre = "h_"
        io8 = dict(ffn_io)
        io8.update({"x": t_x3.ap(), "xo": d_xo})
        build_L4(p=p, io=io8, tag="p8")
        p.phase_barrier()
        p.end_phase()
    return p.finish()


def _l2_tables(rms_w):
    return {
        "tmt8": _bf(np.stack([np.concatenate(
            [np.tile(TMAT_TAB[g * WG + cc].T, (1, UPC)) for cc in range(WG)], 1)
            for g in range(NG)], 0)),
        "id64r": _bf(np.tile(IDENT64, (1, UPC))),
        "arep": _bf(np.broadcast_to(A_TAB.reshape(1, NCHUNK * C), (C, NCHUNK * C))),
        "crep": _bf(np.broadcast_to((1.0 - A_TAB).reshape(1, NCHUNK * C), (C, NCHUNK * C))),
        "wtab": _f32(W_TAB.T),
        "mask01": _bf(MASK_SL), "id64": _bf(IDENT64),
        "rmsw": _bf(np.broadcast_to(np.asarray(rms_w).reshape(1, C), (C, C))),
    }


def fused_host_inputs(core, inp, x_flat, upto=8):
    b, hg, par = core // 2, core % 2, core % 2
    o0 = hg * 512
    wts = {n: np.asarray(inp["L0_attn_a_" + n]) for n in
           ["Wk", "bk", "Wq", "bq", "Wv", "bv", "Wbeta", "bbeta",
            "conv_k", "conv_q", "conv_v", "rms_w", "Wout", "bout"]}
    im = {}
    xs = x_flat[b * L:(b + 1) * L]
    im["xseq"] = _f32(np.concatenate([np.zeros((HALO, D), np.float32), xs], 0).T)
    im["wk8"] = _f32(wts["Wk"].T[:, o0:o0 + 512])
    im["wq8"] = _f32(wts["Wq"].T[:, o0:o0 + 512])
    im["wv8"] = _f32(wts["Wv"].T[:, o0:o0 + 512])
    im["wb8"] = _f32(wts["Wbeta"].T[:, hg * 8:hg * 8 + 8])
    im["bias8"] = _f32(np.concatenate(
        [wts[n][o0:o0 + 512].reshape(4, 128).T for n in ("bk", "bq", "bv")], 1))
    im["bbeta8"] = _f32(wts["bbeta"][hg * 8:hg * 8 + 8].reshape(8, 1))
    conv = np.stack([wts["conv_k"], wts["conv_q"], wts["conv_v"]], 0)
    im["conv8"] = _f32(np.concatenate(
        [np.stack([conv[pi][o0 + jj * 128:o0 + (jj + 1) * 128, tap]
                   for pi in range(3) for tap in range(4)], 1) for jj in range(4)], 1))
    E8 = np.repeat(np.eye(UPC, dtype=np.float32), DH, axis=0)
    im["e8"] = _bf(np.concatenate([E8[jj * 128:(jj + 1) * 128] for jj in range(4)], 1))
    im["e8t"] = _f32(np.concatenate([E8.T[:, jj * 128:(jj + 1) * 128] for jj in range(4)], 1))
    im["id32f_a"] = np.eye(32, dtype=np.float32)
    if upto >= 2:
        im.update(_l2_tables(wts["rms_w"]))
    if upto >= 3:
        oi = np.zeros((128, 32), np.int32)
        for s in range(2):
            for j in range(16):
                oi[:, s * 16 + j] = s * (NCHUNK * C) + par * 2048 + j * 128 + np.arange(128)
        im["oidx"] = oi
        im["x0own"] = _f32(x_flat[core * TOK:(core + 1) * TOK].T)
        im["woT"] = _f32(wts["Wout"].T)
        im["bocol"] = _f32(wts["bout"].reshape(8, 128).T)
    if True:
        im["k0t"] = np.ascontiguousarray(np.asarray(inp["L0_ffn_k0"]).astype(np.uint8)[:D])
        im["k0b"] = np.ascontiguousarray(np.asarray(inp["L0_ffn_k0"]).astype(np.uint8)[D:])
        im["k1"] = np.ascontiguousarray(np.asarray(inp["L0_ffn_k1"]).astype(np.uint8))
        im["k2"] = np.ascontiguousarray(np.asarray(inp["L0_ffn_k2"]).astype(np.uint8))
        im["svals"] = _f32(np.stack([inp["L0_ffn_s0"], inp["L0_ffn_s1"],
                                     inp["L0_ffn_s2"]]).reshape(1, 3))
    if upto >= 5:
        im["wq"] = _f32(np.asarray(inp["L0_attn_b_Wq"]).reshape(D, D))
        im["wk"] = _f32(np.asarray(inp["L0_attn_b_Wk"]).reshape(D, D))
        im["wv"] = _f32(np.asarray(inp["L0_attn_b_Wv"]).reshape(D, D))
        im["whq"] = l5_wh_img(inp["L0_attn_b_Whq"])
        im["whk"] = l5_wh_img(inp["L0_attn_b_Whk"])
        im["gcol"] = _f32(np.asarray(inp["L0_attn_b_ln1_g"]).reshape(8, 128).T)
        im["bcol"] = _f32(np.asarray(inp["L0_attn_b_ln1_b"]).reshape(8, 128).T)
    if upto >= 6:
        im["id32f_b"] = np.eye(32, dtype=np.float32)
    if upto >= 7:
        im["wo"] = _f32(np.asarray(inp["L0_attn_b_Wo"]).reshape(D, D))
        im["g2col"] = _f32(np.asarray(inp["L0_attn_b_ln2_g"]).reshape(8, 128).T)
        im["b2col"] = _f32(np.asarray(inp["L0_attn_b_ln2_b"]).reshape(8, 128).T)
        im["w1"] = _f32(np.asarray(inp["L0_attn_b_W1"]))
        im["b1col"] = _f32(np.asarray(inp["L0_attn_b_b1"]).reshape(32, 128).T)
        im["b1row"] = _f32(np.asarray(inp["L0_attn_b_b1"]).reshape(1, MLPD))
        im["w2"] = _f32(np.asarray(inp["L0_attn_b_W2"]))
        im["bb2col"] = _f32(np.asarray(inp["L0_attn_b_b2"]).reshape(8, 128).T)
    return im


def kernel_fused(**inputs):
    inp = {k: np.asarray(v) for k, v in inputs.items()}
    x0 = inp["L0_attn_a_x"].astype(np.float32, copy=False)
    x_flat = np.ascontiguousarray(x0.reshape(B * L, D))
    cores = list(range(NCORE))
    nc = _prog("FUSED", build_fused)
    in_maps = [fused_host_inputs(c, inp, x_flat) for c in cores]
    res = run_bass_kernel_spmd(nc, in_maps, core_ids=cores)
    x4 = np.concatenate([np.asarray(res.results[c]["xo"]) for c in cores], 1)
    return np.ascontiguousarray(x4.T).reshape(B, L, D).astype(np.float32)


kernel = kernel_fused
